# version 12
# baseline (speedup 1.0000x reference)
from contextlib import ExitStack
import numpy as np
import ml_dtypes
import concourse.bass as bass
import concourse.mybir as mybir
from concourse.alu_op_type import AluOpType as OP
from concourse.bass_utils import run_bass_kernel_spmd

F32 = mybir.dt.float32
BF16 = mybir.dt.bfloat16
ACTF = mybir.ActivationFunctionType
AX = mybir.AxisListType.X

D = 1024
T = 8192
TOWN = 4096
NC = 8
H = 16
HD = 64
TILE_A = 512
NT_A = T // TILE_A
TP = 1024
NPASS = TOWN // TP
TT = 512
NTT = TP // TT
BIG = 30000.0

ENGS = ("sync", "gpsimd", "tensor", "vector", "scalar")

_DBG = False


class Em:

    def __init__(self, nc, ctx):
        self.nc = nc
        self.ctx = ctx
        self.ops = {e: [] for e in ENGS}
        self.sem = {}
        self.cnt = {}

    def new_sem(self, name):
        self.sem[name] = self.ctx.enter_context(self.nc.semaphore(name))
        self.cnt[name] = 0

    def emit(self, eng, fn, inc=None, k=1):
        if inc is None:
            self.ops[eng].append(fn)
        else:
            h = self.sem[inc]
            self.cnt[inc] += k
            self.ops[eng].append(lambda e, fn=fn, h=h, k=k: fn(e).then_inc(h, k))

    def wait(self, eng, name, val=None):
        v = self.cnt[name] if val is None else val
        if v <= 0:
            return
        assert v <= self.cnt[name], f"wait {name} {v} > emitted {self.cnt[name]}"
        h = self.sem[name]
        self.ops[eng].append(lambda e, h=h, v=v: e.wait_ge(h, v))

    def val(self, name):
        return self.cnt[name]

    def nop_inc(self, eng, name, k=1):
        h = self.sem[name]
        self.cnt[name] += k
        self.ops[eng].append(lambda e, h=h, k=k: e.drain().then_inc(h, k))

    def replay(self, blk):
        for eng in ENGS:
            dec = getattr(blk, eng)
            ops = self.ops[eng]

            def body(engine, ops=ops):
                for op in ops:
                    op(engine)
            dec(body)
        self.ops = {e: [] for e in ENGS}


def ap3(t, off, dims):
    return bass.AP(t, off, dims)


def build_program():
    nc = bass.Bass("TRN2", target_bir_lowering=False, debug=False)

    X = nc.dram_tensor("x", [D, T], F32, kind="ExternalInput")
    WQKV = nc.dram_tensor("wqkv", [D, 3 * D], F32, kind="ExternalInput")
    WOUT = nc.dram_tensor("wout", [D, D], F32, kind="ExternalInput")
    W1T = nc.dram_tensor("w1t", [8, D, 4 * D], F32, kind="ExternalInput")
    W2T = nc.dram_tensor("w2t", [8, 2 * D, D], F32, kind="ExternalInput")
    WRT = nc.dram_tensor("wrt", [D, 8], F32, kind="ExternalInput")
    COST = nc.dram_tensor("cost", [T, HD], F32, kind="ExternalInput")
    SINT = nc.dram_tensor("sint", [T, HD], F32, kind="ExternalInput")
    COSF = nc.dram_tensor("cosf", [128, TOWN], F32, kind="ExternalInput")
    SINF = nc.dram_tensor("sinf", [128, TOWN], F32, kind="ExternalInput")
    SEL16 = nc.dram_tensor("sel16", [16, NC * 128], BF16, kind="ExternalInput")
    SEL8 = nc.dram_tensor("sel8", [8, 8 * 128], BF16, kind="ExternalInput")
    ONES1 = nc.dram_tensor("ones1", [1, 128], F32, kind="ExternalInput")
    ONESB = nc.dram_tensor("onesb", [128, 1], BF16, kind="ExternalInput")
    IDF = nc.dram_tensor("idf", [128, 128], F32, kind="ExternalInput")
    EPSV = nc.dram_tensor("epsv", [128, 1], F32, kind="ExternalInput")
    XNS = nc.dram_tensor("xns", [D, TOWN], BF16, kind="Internal")
    OUT = nc.dram_tensor("out", [D, TOWN], F32, kind="ExternalOutput")
    if _DBG:
        DBG_G = nc.dram_tensor("dbg_g", [8, 512], F32, kind="ExternalOutput")
        DBG_CT = nc.dram_tensor("dbg_ct", [128, 32], F32, kind="ExternalOutput")
        DBG_CB = nc.dram_tensor("dbg_cb", [8, NTT * 512], F32, kind="ExternalOutput")
        DBG_ES = nc.dram_tensor("dbg_es", [128, 32], F32, kind="ExternalOutput")
        DBG_SUM = nc.dram_tensor("dbg_sum", [128, 16], F32, kind="ExternalOutput")

    with ExitStack() as ctx0:
        em = Em(nc, ctx0)
        for s in ("s_x", "s_xb", "s_tab", "s_xns", "s_cst",
                  "s_sq", "s_mmst", "s_vst", "s_rstd", "s_mmbc", "s_xn",
                  "s_mmkp", "s_evkp", "s_krV", "s_eS", "s_kfV", "s_mmkv",
                  "sP", "sV", "sS", "s_kvfin",
                  "s_xnp", "s_accx", "s_cf", "s_wq", "s_wo", "s_wr",
                  "s_w1", "s_w2",
                  "s_qmm", "s_qf", "s_eB", "s_krB", "s_den", "s_rd", "s_bc", "s_bcev", "s_qs",
                  "s_att", "s_attev", "s_wt", "s_acA",
                  "s_rt", "s_gsb", "s_trf", "s_smx", "s_trb", "s_comb",
                  "s_bcePE", "s_bceS", "s_gmm", "s_silu", "s_y", "s_w2mm", "s_acF",
                  "s_out", "sPB", "sVB", "s_dbg", "s_rsv"):
            em.new_sem(s)

        kvbd = ctx0.enter_context(nc.sbuf_tensor("kvbd", [128, 8 * 128], BF16))
        ksbd = ctx0.enter_context(nc.sbuf_tensor("ksbd", [128, 8 * 16], BF16))
        sel16_sb = ctx0.enter_context(nc.sbuf_tensor("sel16_sb", [16, NC * 128], BF16))
        sel8_sb = ctx0.enter_context(nc.sbuf_tensor("sel8_sb", [8, 8 * 128], BF16))
        ones1_sb = ctx0.enter_context(nc.sbuf_tensor("ones1_sb", [1, 128], F32))
        onesb_sb = ctx0.enter_context(nc.sbuf_tensor("onesb_sb", [128, 1], BF16))
        idf_sb = ctx0.enter_context(nc.sbuf_tensor("idf_sb", [128, 128], F32))
        epsv_sb = ctx0.enter_context(nc.sbuf_tensor("epsv_sb", [128, 1], F32))

        def load_consts(e):
            pass
        for dst, src in [(sel16_sb[:, :], SEL16.ap()), (sel8_sb[:, :], SEL8.ap()),
                         (ones1_sb[:, :], ONES1.ap()), (onesb_sb[:, :], ONESB.ap()),
                         (idf_sb[:, :], IDF.ap()), (epsv_sb[:, :], EPSV.ap())]:
            em.emit("sync", lambda e, d=dst, s=src: e.dma_start(d, s), inc="s_cst", k=16)

        with ExitStack() as ctxA:
            xf = [ctxA.enter_context(nc.sbuf_tensor(f"xf{j}", [128, NC * TILE_A], F32)) for j in range(2)]
            xbf = [ctxA.enter_context(nc.sbuf_tensor(f"xbf{j}", [128, NC * TILE_A], BF16)) for j in range(2)]
            sq_sb = ctxA.enter_context(nc.sbuf_tensor("sq_sb", [128, NC * TILE_A], BF16))
            xn_t = [ctxA.enter_context(nc.sbuf_tensor(f"xnt{j}", [128, NC * TILE_A], BF16)) for j in range(2)]
            ct_sb = [ctxA.enter_context(nc.sbuf_tensor(f"ct{j}", [128, 4 * HD], F32)) for j in range(2)]
            st_sb = [ctxA.enter_context(nc.sbuf_tensor(f"st{j}", [128, 4 * HD], F32)) for j in range(2)]
            wqkv_sb = ctxA.enter_context(nc.sbuf_tensor("wqkv_sb", [128, NC * 3 * D], BF16))
            k_tok = ctxA.enter_context(nc.sbuf_tensor("k_tok", [128, 2 * 1024], BF16))
            v_tok = ctxA.enter_context(nc.sbuf_tensor("v_tok", [128, 2 * 1024], BF16))
            rotA = ctxA.enter_context(nc.sbuf_tensor("rotA", [128, 512], F32))
            t1A = ctxA.enter_context(nc.sbuf_tensor("t1A", [128, 512], F32))
            t2A = ctxA.enter_context(nc.sbuf_tensor("t2A", [128, 512], F32))
            rAs = ctxA.enter_context(nc.sbuf_tensor("rAs", [128, 512], F32))
            eAs = ctxA.enter_context(nc.sbuf_tensor("eAs", [128, 512], F32))
            mean_sb = ctxA.enter_context(nc.sbuf_tensor("mean_sb", [1, 512], F32))
            ms_sb = ctxA.enter_context(nc.sbuf_tensor("ms_sb", [1, 512], F32))
            var_sb = ctxA.enter_context(nc.sbuf_tensor("var_sb", [1, 512], F32))
            rstd_sb = ctxA.enter_context(nc.sbuf_tensor("rstd_sb", [1, 512], F32))

            kvacc = ctxA.enter_context(nc.psum_tensor("kvacc", [128, 1024], F32))
            ksum = ctxA.enter_context(nc.psum_tensor("ksum", [128, 512], F32))
            stats = ctxA.enter_context(nc.psum_tensor("stats", [128, 512], F32))
            meanbc = ctxA.enter_context(nc.psum_tensor("meanbc", [128, 512], F32))
            rstdbc = ctxA.enter_context(nc.psum_tensor("rstdbc", [128, 512], F32))
            kprod = ctxA.enter_context(nc.psum_tensor("kprod", [128, 1024], F32))

            em.emit("gpsimd", lambda e: e.dma_start(
                ap3(wqkv_sb, 0, [[NC * 3 * D, 128], [3 * D, NC], [1, 3 * D]]),
                ap3(WQKV, 0, [[3 * D, 128], [128 * 3 * D, NC], [1, 3 * D]])), inc="s_xb", k=16)

            em.wait("tensor", "s_cst", 16 * 6)
            em.wait("scalar", "s_cst", 16 * 6)
            kp_g = 0
            kf_i = 0
            cg = 0
            nxns = 0

            for i in range(NT_A):
                b = i % 2
                own = i < NT_A // 2
                if i >= 2:
                    for s in ("sP", "sV", "sS"):
                        em.wait("sync", s, i - 1)
                        em.wait("gpsimd", s, i - 1)
                em.emit("sync", lambda e, i=i, b=b: e.dma_start(
                    ap3(xf[b], 0, [[NC * TILE_A, 128], [TILE_A, NC], [1, TILE_A]]),
                    ap3(X, i * TILE_A, [[T, 128], [128 * T, NC], [1, TILE_A]])), inc="s_x", k=16)
                em.emit("sync", lambda e, i=i, b=b: e.dma_start(
                    ap3(ct_sb[b], 0, [[4 * HD, 128], [HD, 4], [1, HD]]),
                    ap3(COST, i * TILE_A * HD, [[HD, 128], [128 * HD, 4], [1, HD]])), inc="s_tab", k=16)
                em.emit("sync", lambda e, i=i, b=b: e.dma_start(
                    ap3(st_sb[b], 0, [[4 * HD, 128], [HD, 4], [1, HD]]),
                    ap3(SINT, i * TILE_A * HD, [[HD, 128], [128 * HD, 4], [1, HD]])), inc="s_tab", k=16)
                em.emit("gpsimd", lambda e, i=i, b=b: e.dma_start(
                    ap3(xbf[b], 0, [[NC * TILE_A, 128], [TILE_A, NC], [1, TILE_A]]),
                    ap3(X, i * TILE_A, [[T, 128], [128 * T, NC], [1, TILE_A]])), inc="s_xb", k=16)

                em.wait("scalar", "s_x")
                if i >= 1:
                    em.wait("scalar", "sP", i)
                for c in range(NC):
                    em.emit("scalar", lambda e, b=b, c=c: e.activation(
                        sq_sb[:, c * 512:(c + 1) * 512], xf[b][:, c * 512:(c + 1) * 512], ACTF.Square),
                        inc="s_sq" if c == NC - 1 else None)

                em.wait("tensor", "s_xb")
                em.wait("tensor", "s_sq")
                if i >= 1:
                    em.wait("tensor", "sV", i)
                for c in range(NC):
                    em.emit("tensor", lambda e, b=b, c=c: e.matmul(
                        stats[0:1, :], onesb_sb[:, :], xbf[b][:, c * 512:(c + 1) * 512],
                        start=(c == 0), stop=(c == NC - 1)))
                for c in range(NC):
                    em.emit("tensor", lambda e, c=c: e.matmul(
                        stats[32:33, :], onesb_sb[:, :], sq_sb[:, c * 512:(c + 1) * 512],
                        start=(c == 0), stop=(c == NC - 1)),
                        inc="s_mmst" if c == NC - 1 else None)

                em.wait("vector", "s_mmst")
                em.emit("vector", lambda e: e.tensor_scalar(mean_sb[:, :], stats[0:1, :], 1.0 / D, None, OP.mult))
                em.emit("vector", lambda e: e.tensor_tensor(ms_sb[:, :], mean_sb[:, :], mean_sb[:, :], OP.mult))
                em.emit("vector", lambda e: e.scalar_tensor_tensor(
                    var_sb[:, :], stats[32:33, :], 1.0 / D, ms_sb[:, :], OP.mult, OP.subtract), inc="s_vst")
                em.wait("scalar", "s_vst")
                em.emit("scalar", lambda e: e.activation(rstd_sb[:, :], var_sb[:, :], ACTF.Ln, bias=epsv_sb[0:1, 0:1]))
                em.emit("scalar", lambda e: e.activation(rstd_sb[:, :], rstd_sb[:, :], ACTF.Exp, scale=-0.5), inc="s_rstd")

                em.wait("tensor", "s_rstd")
                em.wait("tensor", "s_cst", 16 * 6)
                em.emit("tensor", lambda e: e.matmul(meanbc[:, :], ones1_sb[:, :], mean_sb[:, :], start=True, stop=True))
                em.emit("tensor", lambda e: e.matmul(rstdbc[:, :], ones1_sb[:, :], rstd_sb[:, :], start=True, stop=True),
                        inc="s_mmbc")

                em.wait("vector", "s_mmbc")
                if i >= 2:
                    em.wait("vector", "sP", i - 1)
                    em.wait("vector", "s_xns", 16 * min(i - 1, 8))
                for c in range(NC):
                    em.emit("vector", lambda e, b=b, c=c: e.tensor_tensor(
                        t1A[:, :], xf[b][:, c * 512:(c + 1) * 512], meanbc[:, :], OP.subtract))
                    em.emit("vector", lambda e, b=b, c=c: e.tensor_tensor(
                        xn_t[b][:, c * 512:(c + 1) * 512], t1A[:, :], rstdbc[:, :], OP.mult),
                        inc="s_xn" if c == NC - 1 else None)

                if own:
                    em.wait("sync", "s_xn", em.val("s_xn"))
                    em.emit("sync", lambda e, i=i, b=b: e.dma_start(
                        ap3(XNS, i * TILE_A, [[TOWN, 128], [128 * TOWN, NC], [1, TILE_A]]),
                        ap3(xn_t[b], 0, [[NC * TILE_A, 128], [TILE_A, NC], [1, TILE_A]])), inc="s_xns", k=16)
                    nxns += 1

                em.wait("tensor", "s_xn")
                em.wait("tensor", "s_tab")
                for cc in range(4):
                    kb = cc % 2
                    for f in range(4):
                        bank = kp_g % 2
                        if kp_g >= 2:
                            em.wait("tensor", "s_evkp", kp_g - 1)
                        for c in range(NC):
                            em.emit("tensor", lambda e, b=b, c=c, cc=cc, f=f, bank=bank: e.matmul(
                                kprod[:, bank * 512:(bank + 1) * 512],
                                xn_t[b][:, c * 512 + cc * 128: c * 512 + cc * 128 + 128],
                                wqkv_sb[:, c * 3 * D + D + f * 512: c * 3 * D + D + (f + 1) * 512],
                                start=(c == 0), stop=(c == NC - 1)),
                                inc="s_mmkp" if c == NC - 1 else None)
                        if f >= 2:
                            em.wait("scalar", "s_mmkp", kp_g + 1)
                            if cg >= 2:
                                em.wait("scalar", "s_mmkv", cg - 1)
                            em.emit("scalar", lambda e, kb=kb, f=f, bank=bank: e.copy(
                                v_tok[:, kb * 1024 + (f - 2) * 512: kb * 1024 + (f - 1) * 512],
                                kprod[:, bank * 512:(bank + 1) * 512]), inc="s_evkp")
                        else:
                            em.wait("vector", "s_mmkp", kp_g + 1)
                            if kf_i >= 1:
                                em.wait("vector", "s_eS", kf_i)
                            em.emit("vector", lambda e, bank=bank: e.tensor_scalar(
                                ap3(rotA, 0, [[512, 128], [64, 8], [1, 32]]),
                                ap3(kprod, bank * 512 + 32, [[1024, 128], [64, 8], [1, 32]]),
                                -1.0, None, OP.mult))
                            em.emit("vector", lambda e, bank=bank: e.tensor_copy(
                                ap3(rotA, 32, [[512, 128], [64, 8], [1, 32]]),
                                ap3(kprod, bank * 512, [[1024, 128], [64, 8], [1, 32]])))
                            em.emit("vector", lambda e, b=b, cc=cc, bank=bank: e.tensor_tensor(
                                ap3(t1A, 0, [[512, 128], [64, 8], [1, 64]]),
                                ap3(kprod, bank * 512, [[1024, 128], [64, 8], [1, 64]]),
                                ap3(ct_sb[b], cc * HD, [[4 * HD, 128], [0, 8], [1, 64]]),
                                OP.mult), inc="s_evkp")
                            em.emit("vector", lambda e, b=b, cc=cc: e.tensor_tensor(
                                ap3(t2A, 0, [[512, 128], [64, 8], [1, 64]]),
                                ap3(rotA, 0, [[512, 128], [64, 8], [1, 64]]),
                                ap3(st_sb[b], cc * HD, [[4 * HD, 128], [0, 8], [1, 64]]),
                                OP.mult))
                            em.emit("vector", lambda e: e.tensor_tensor(t1A[:, :], t1A[:, :], t2A[:, :], OP.add))
                            em.emit("vector", lambda e: e.tensor_scalar(rAs[:, :], t1A[:, :], 0.0, None, OP.max),
                                    inc="s_krV")
                            em.wait("scalar", "s_krV")
                            if kf_i >= 1:
                                em.wait("scalar", "s_kfV", kf_i)
                            em.emit("scalar", lambda e: e.activation(eAs[:, :], t1A[:, :], ACTF.Exp), inc="s_eS")
                            em.wait("vector", "s_eS")
                            if cg >= 2:
                                em.wait("vector", "s_mmkv", cg - 1)
                            em.emit("vector", lambda e, kb=kb, f=f: e.scalar_tensor_tensor(
                                k_tok[:, kb * 1024 + f * 512: kb * 1024 + (f + 1) * 512],
                                eAs[:, :], 1.0, rAs[:, :], OP.min, OP.add), inc="s_kfV")
                            kf_i += 1
                        kp_g += 1
                    em.wait("tensor", "s_kfV", 2 * (cg + 1))
                    em.wait("tensor", "s_evkp", em.val("s_evkp"))
                    last_mm_of_tile = (cc == 3)
                    for p in range(8):
                        st_kv = (cg == 0 and p in (0, 4))
                        st_ks = (cg == 0 and p == 0)
                        fin = (cg == 63)
                        em.emit("tensor", lambda e, kb=kb, p=p, st=st_kv, fin=fin: e.matmul(
                            kvacc[:, p * 128:(p + 1) * 128],
                            k_tok[:, kb * 1024 + p * 128: kb * 1024 + (p + 1) * 128],
                            v_tok[:, kb * 1024 + p * 128: kb * 1024 + (p + 1) * 128],
                            start=st, stop=fin, skip_group_check=True))
                        em.emit("tensor", lambda e, kb=kb, p=p, st=st_ks, fin=fin: e.matmul(
                            ksum[:, p:p + 1],
                            k_tok[:, kb * 1024 + p * 128: kb * 1024 + (p + 1) * 128],
                            onesb_sb[:, :],
                            start=st, stop=(fin and p == 7), skip_group_check=True),
                            inc="s_mmkv" if p == 7 else None)
                    cg += 1
                em.nop_inc("tensor", "sP")
                em.nop_inc("vector", "sV")
                em.nop_inc("scalar", "sS")

            em.wait("vector", "sP", NT_A)
            em.emit("vector", lambda e: e.memset(kvbd[:, :], 0.0))
            em.emit("vector", lambda e: e.memset(ksbd[:, :], 0.0))
            for p in range(8):
                em.emit("vector", lambda e, p=p: e.tensor_copy(
                    kvbd[0:64, p * 128: p * 128 + 64], kvacc[0:64, p * 128: p * 128 + 64]))
                em.emit("vector", lambda e, p=p: e.tensor_copy(
                    kvbd[64:128, p * 128 + 64: p * 128 + 128], kvacc[64:128, p * 128 + 64: p * 128 + 128]))
                em.emit("vector", lambda e, p=p: e.tensor_copy(
                    ksbd[0:64, p * 16 + 2 * p: p * 16 + 2 * p + 1], ksum[0:64, p:p + 1]))
                em.emit("vector", lambda e, p=p: e.tensor_copy(
                    ksbd[64:128, p * 16 + 2 * p + 1: p * 16 + 2 * p + 2], ksum[64:128, p:p + 1]),
                    inc="s_kvfin" if p == 7 else None)
            em.wait("sync", "s_xns", 16 * 8)
            em.wait("sync", "s_x")

            with nc.Block() as blk:
                em.replay(blk)

        with ExitStack() as ctxB:
            xnp = ctxB.enter_context(nc.sbuf_tensor("xnp", [128, NC * TP], BF16))
            accb = ctxB.enter_context(nc.sbuf_tensor("accb", [128, NC * TP], F32))
            w1h = [ctxB.enter_context(nc.sbuf_tensor(f"w1h{j}", [128, 16384], BF16)) for j in range(2)]
            w2h = ctxB.enter_context(nc.sbuf_tensor("w2h", [128, 8 * 1024], BF16))
            y_sb = ctxB.enter_context(nc.sbuf_tensor("y_sb", [128, 8 * NTT * 512], BF16))
            qf_bf = ctxB.enter_context(nc.sbuf_tensor("qf_bf", [128, NC * 512], BF16))
            rotB = ctxB.enter_context(nc.sbuf_tensor("rotB", [128, 512], F32))
            t1B = ctxB.enter_context(nc.sbuf_tensor("t1B", [128, 512], F32))
            t2B = ctxB.enter_context(nc.sbuf_tensor("t2B", [128, 512], F32))
            rBs = ctxB.enter_context(nc.sbuf_tensor("rBs", [128, 512], F32))
            eBs = ctxB.enter_context(nc.sbuf_tensor("eBs", [128, 512], F32))
            wr_sb = ctxB.enter_context(nc.sbuf_tensor("wr_sb", [128, NC * 8], BF16))
            cfp = ctxB.enter_context(nc.sbuf_tensor("cfp", [128, TP], F32))
            sfp = ctxB.enter_context(nc.sbuf_tensor("sfp", [128, TP], F32))
            rd_f = ctxB.enter_context(nc.sbuf_tensor("rd_f", [16, 512], F32))
            rd_bf = ctxB.enter_context(nc.sbuf_tensor("rd_bf", [16, 512], BF16))
            bcq_bf = ctxB.enter_context(nc.sbuf_tensor("bcq_bf", [128, 512], BF16))
            gsb = ctxB.enter_context(nc.sbuf_tensor("gsb", [8, 512], F32))
            m1_c = ctxB.enter_context(nc.sbuf_tensor("m1_c", [128, 8], F32))
            pen_c = ctxB.enter_context(nc.sbuf_tensor("pen_c", [128, 8], F32))
            msk_c = ctxB.enter_context(nc.sbuf_tensor("msk_c", [128, 8], F32))
            es_c = ctxB.enter_context(nc.sbuf_tensor("es_c", [128, 8], F32))
            es2_c = ctxB.enter_context(nc.sbuf_tensor("es2_c", [128, 8], F32))
            es_all = ctxB.enter_context(nc.sbuf_tensor("es_all", [128, 32], F32))
            m1s = ctxB.enter_context(nc.sbuf_tensor("m1s", [128, 8], F32))
            sums8 = ctxB.enter_context(nc.sbuf_tensor("sums8", [128, 8], F32))
            cu_sb = ctxB.enter_context(nc.sbuf_tensor("cu_sb", [8, 512], F32))
            pen_all = ctxB.enter_context(nc.sbuf_tensor("pen_all", [128, 32], F32))
            pba = ctxB.enter_context(nc.sbuf_tensor("pba", [128, 32], F32))
            msk_all = ctxB.enter_context(nc.sbuf_tensor("msk_all", [128, 32], F32))
            m2s8 = ctxB.enter_context(nc.sbuf_tensor("m2s8", [128, 4], F32))
            es2_all = ctxB.enter_context(nc.sbuf_tensor("es2_all", [128, 32], F32))
            bigneg = ctxB.enter_context(nc.sbuf_tensor("bigneg", [128, 8], F32))
            srow_sb = ctxB.enter_context(nc.sbuf_tensor("srow_sb", [1, 512], F32))
            rs_sb = ctxB.enter_context(nc.sbuf_tensor("rs_sb", [1, 512], F32))
            sum_c = ctxB.enter_context(nc.sbuf_tensor("sum_c", [128, 4], F32))
            combtok = ctxB.enter_context(nc.sbuf_tensor("combtok", [128, 32], F32))
            dbges = ctxB.enter_context(nc.sbuf_tensor("dbges", [128, 32], F32))
            dbgsum = ctxB.enter_context(nc.sbuf_tensor("dbgsum", [128, 16], F32))
            comb_bf = ctxB.enter_context(nc.sbuf_tensor("comb_bf", [8, NTT * 512], BF16))
            bce_sb = ctxB.enter_context(nc.sbuf_tensor("bce_sb", [128, NTT * 512], F32))
            silu_sb = ctxB.enter_context(nc.sbuf_tensor("silu_sb", [128, 512], F32))
            tmf = ctxB.enter_context(nc.sbuf_tensor("tmf", [128, 512], F32))

            P0 = ctxB.enter_context(nc.psum_tensor("P0", [128, 1024], F32))
            P1 = ctxB.enter_context(nc.psum_tensor("P1", [128, 1024], F32))
            P2 = ctxB.enter_context(nc.psum_tensor("P2", [128, 1024], F32))
            P3 = ctxB.enter_context(nc.psum_tensor("P3", [128, 1024], F32))

            def BANK(k):
                return ((P0, P1, P2, P3)[k // 2], (k % 2) * 512)

            wq_sb = w1h[1]
            WQ_OFF, WO_OFF = 0, 8192

            w1_loads = 0
            w1_last_use = {0: 0, 1: 0}
            w2_loads = 0
            hmm_cnt = 0
            w2mm_cnt = 0

            def load_wqwo():
                em.emit("gpsimd", lambda e: e.dma_start(
                    ap3(wq_sb, WQ_OFF, [[16384, 128], [1024, NC], [1, 1024]]),
                    ap3(WQKV, 0, [[3 * D, 128], [128 * 3 * D, NC], [1, 1024]])), inc="s_wq", k=16)
                em.emit("gpsimd", lambda e: e.dma_start(
                    ap3(wq_sb, WO_OFF, [[16384, 128], [1024, NC], [1, 1024]]),
                    ap3(WOUT, 0, [[D, 128], [128 * D, NC], [1, 1024]])), inc="s_wo", k=16)

            em.emit("vector", lambda e: e.memset(bigneg[:, :], -BIG))
            em.emit("gpsimd", lambda e: e.dma_start(
                ap3(wr_sb, 0, [[NC * 8, 128], [8, NC], [1, 8]]),
                ap3(WRT, 0, [[8, 128], [128 * 8, NC], [1, 8]])), inc="s_wr", k=16)
            load_wqwo()

            for p in range(NPASS):
                if p >= 1:
                    em.wait("sync", "sPB", p)
                    em.wait("sync", "sVB", p)
                    em.wait("sync", "s_out", 16 * p)
                em.emit("sync", lambda e, p=p: e.dma_start(
                    ap3(xnp, 0, [[NC * TP, 128], [TP, NC], [1, TP]]),
                    ap3(XNS, p * TP, [[TOWN, 128], [128 * TOWN, NC], [1, TP]])), inc="s_xnp", k=16)
                em.emit("sync", lambda e, p=p: e.dma_start(
                    ap3(accb, 0, [[NC * TP, 128], [TP, NC], [1, TP]]),
                    ap3(X, p * TP, [[T, 128], [128 * T, NC], [1, TP]])), inc="s_accx", k=16)
                em.emit("sync", lambda e, p=p: e.dma_start(
                    cfp[:, :], ap3(COSF, p * TP, [[TOWN, 128], [1, TP]])), inc="s_cf", k=16)
                em.emit("sync", lambda e, p=p: e.dma_start(
                    sfp[:, :], ap3(SINF, p * TP, [[TOWN, 128], [1, TP]])), inc="s_cf", k=16)

                em.wait("tensor", "s_kvfin", 1)
                em.wait("tensor", "s_y", em.val("s_y"))
                em.wait("tensor", "s_acF", em.val("s_acF"))
                em.wait("tensor", "s_bceS", em.val("s_bceS"))
                for tt in range(NTT):
                    em.wait("tensor", "s_xnp")
                    em.wait("tensor", "s_wq")
                    for c in range(NC):
                        qb = c % 2
                        if em.val("s_qmm") >= 2:
                            em.wait("tensor", "s_qf", em.val("s_qmm") - 1)
                        for k in range(NC):
                            em.emit("tensor", lambda e, c=c, k=k, tt=tt, qb=qb: e.matmul(
                                BANK(qb)[0][:, BANK(qb)[1]:BANK(qb)[1] + 512],
                                wq_sb[:, WQ_OFF + k * 1024 + c * 128: WQ_OFF + k * 1024 + (c + 1) * 128],
                                xnp[:, k * TP + tt * 512: k * TP + (tt + 1) * 512],
                                start=(k == 0), stop=(k == NC - 1)),
                                inc="s_qmm" if k == NC - 1 else None)
                        em.wait("vector", "s_qmm")
                        em.wait("vector", "s_cf")
                        if em.val("s_qf") >= 1:
                            em.wait("vector", "s_eB", em.val("s_qf"))
                        pst, pco = BANK(qb)
                        em.emit("vector", lambda e, pst=pst, pco=pco: e.tensor_scalar(
                            rotB[0:32, :], pst[32:64, pco:pco + 512], -1.0, None, OP.mult))
                        em.emit("vector", lambda e, pst=pst, pco=pco: e.tensor_copy(
                            rotB[32:64, :], pst[0:32, pco:pco + 512]))
                        em.emit("vector", lambda e, pst=pst, pco=pco: e.tensor_scalar(
                            rotB[64:96, :], pst[96:128, pco:pco + 512], -1.0, None, OP.mult))
                        em.emit("vector", lambda e, pst=pst, pco=pco: e.tensor_copy(
                            rotB[96:128, :], pst[64:96, pco:pco + 512]))
                        em.emit("vector", lambda e, pst=pst, pco=pco, tt=tt: e.tensor_tensor(
                            t1B[:, :], pst[:, pco:pco + 512], cfp[:, tt * 512:(tt + 1) * 512], OP.mult))
                        em.emit("vector", lambda e, tt=tt: e.tensor_tensor(
                            t2B[:, :], rotB[:, :], sfp[:, tt * 512:(tt + 1) * 512], OP.mult))
                        em.emit("vector", lambda e: e.tensor_tensor(t1B[:, :], t1B[:, :], t2B[:, :], OP.add))
                        em.emit("vector", lambda e: e.tensor_scalar(
                            rBs[:, :], t1B[:, :], 0.125, 0.0, OP.mult, OP.max), inc="s_krB")
                        em.wait("scalar", "s_krB", em.val("s_krB"))
                        if em.val("s_eB") >= 1:
                            em.wait("scalar", "s_qf", em.val("s_eB"))
                        em.emit("scalar", lambda e: e.activation(eBs[:, :], t1B[:, :], ACTF.Exp, scale=0.125),
                                inc="s_eB")
                        em.wait("vector", "s_eB")
                        em.emit("vector", lambda e, c=c: e.scalar_tensor_tensor(
                            qf_bf[:, c * 512:(c + 1) * 512], eBs[:, :], 1.0, rBs[:, :], OP.min, OP.add),
                            inc="s_qf")
                    em.wait("tensor", "s_qf")
                    if p + tt > 0:
                        em.wait("tensor", "s_rd", em.val("s_rd"))
                        em.wait("tensor", "s_gsb", em.val("s_gsb"))
                        em.wait("tensor", "s_comb", em.val("s_comb"))
                    for pr in range(8):
                        em.emit("tensor", lambda e, pr=pr: e.matmul(
                            P1[0:16, 0:512], ksbd[:, pr * 16:(pr + 1) * 16],
                            qf_bf[:, pr * 512:(pr + 1) * 512],
                            start=(pr == 0), stop=(pr == 7)),
                            inc="s_den" if pr == 7 else None)
                    em.wait("vector", "s_den")
                    em.emit("vector", lambda e: e.tensor_scalar(rd_f[:, :], P1[0:16, 0:512], 1e-6, None, OP.max))
                    em.emit("vector", lambda e: e.reciprocal(rd_f[:, :], rd_f[:, :]))
                    em.emit("vector", lambda e: e.tensor_copy(rd_bf[:, :], rd_f[:, :]), inc="s_rd")
                    em.wait("tensor", "s_smx", em.val("s_smx"))
                    em.wait("tensor", "s_comb", em.val("s_comb"))
                    for c in range(NC):
                        em.wait("tensor", "s_rd", em.val("s_rd"))
                        if em.val("s_bc") >= 1:
                            em.wait("tensor", "s_bcev", em.val("s_bc"))
                        em.emit("tensor", lambda e, c=c: e.matmul(
                            P1[:, 512:1024], sel16_sb[:, c * 128:(c + 1) * 128], rd_bf[:, :],
                            start=True, stop=True), inc="s_bc")
                        em.wait("scalar", "s_bc", em.val("s_bc"))
                        if em.val("s_bcev") >= 1:
                            em.wait("scalar", "s_qs", em.val("s_bcev"))
                        em.emit("scalar", lambda e: e.copy(bcq_bf[:, :], P1[:, 512:1024]), inc="s_bcev")
                        em.wait("vector", "s_bcev", em.val("s_bcev"))
                        em.emit("vector", lambda e, c=c: e.tensor_tensor(
                            qf_bf[:, c * 512:(c + 1) * 512], qf_bf[:, c * 512:(c + 1) * 512],
                            bcq_bf[:, :], OP.mult), inc="s_qs")
                    em.wait("tensor", "s_qs")
                    em.wait("tensor", "s_kvfin", 1)
                    for pr in range(8):
                        ab = 4 + pr % 2
                        if em.val("s_att") >= 2:
                            em.wait("tensor", "s_attev", em.val("s_att") - 1)
                        pst, pco = BANK(ab)
                        em.emit("tensor", lambda e, pr=pr, pst=pst, pco=pco: e.matmul(
                            pst[:, pco:pco + 512], kvbd[:, pr * 128:(pr + 1) * 128],
                            qf_bf[:, pr * 512:(pr + 1) * 512], start=True, stop=True), inc="s_att")
                        em.wait("scalar", "s_att", em.val("s_att"))
                        em.emit("scalar", lambda e, pr=pr, pst=pst, pco=pco: e.copy(
                            y_sb[:, pr * 512:(pr + 1) * 512], pst[:, pco:pco + 512]), inc="s_attev")
                    em.wait("tensor", "s_attev")
                    em.wait("tensor", "s_wo")
                    for m in range(NC):
                        wb = 6 + m % 2
                        if em.val("s_wt") >= 2:
                            em.wait("tensor", "s_acA", em.val("s_wt") - 1)
                        pst, pco = BANK(wb)
                        for k in range(NC):
                            em.emit("tensor", lambda e, m=m, k=k, pst=pst, pco=pco: e.matmul(
                                pst[:, pco:pco + 512],
                                wq_sb[:, WO_OFF + k * 1024 + m * 128: WO_OFF + k * 1024 + (m + 1) * 128],
                                y_sb[:, k * 512:(k + 1) * 512],
                                start=(k == 0), stop=(k == NC - 1)),
                                inc="s_wt" if k == NC - 1 else None)
                        em.wait("vector", "s_wt", em.val("s_wt"))
                        em.wait("vector", "s_accx")
                        em.emit("vector", lambda e, m=m, tt=tt, pst=pst, pco=pco: e.tensor_tensor(
                            accb[:, m * TP + tt * 512: m * TP + (tt + 1) * 512],
                            accb[:, m * TP + tt * 512: m * TP + (tt + 1) * 512],
                            pst[:, pco:pco + 512], OP.add), inc="s_acA")
                    em.wait("tensor", "s_wr")
                    em.wait("tensor", "s_rd", em.val("s_rd"))
                    if em.val("s_comb") >= 1:
                        em.wait("tensor", "s_comb", em.val("s_comb"))
                    for k in range(NC):
                        em.emit("tensor", lambda e, k=k, tt=tt: e.matmul(
                            P1[0:8, 0:512], wr_sb[:, k * 8:(k + 1) * 8],
                            xnp[:, k * TP + tt * 512: k * TP + (tt + 1) * 512],
                            start=(k == 0), stop=(k == NC - 1)),
                            inc="s_rt" if k == NC - 1 else None)
                    em.wait("vector", "s_rt")
                    em.emit("vector", lambda e: e.tensor_copy(gsb[:, :], P1[0:8, 0:512]), inc="s_gsb")
                    em.wait("tensor", "s_gsb")
                    em.wait("tensor", "s_bcev", em.val("s_bcev"))
                    for j in range(4):
                        em.emit("tensor", lambda e, j=j: e.transpose(
                            P1[:, 512 + j * 8: 512 + (j + 1) * 8],
                            gsb[:, j * 128:(j + 1) * 128], idf_sb[0:8, 0:8]),
                            inc="s_trf" if j == 3 else None)
                    em.wait("vector", "s_trf")
                    for j in range(4):
                        gj = (P1, 512 + j * 8)
                        em.emit("vector", lambda e, j=j, gj=gj: e.tensor_reduce(
                            m1s[:, 2 * j:2 * j + 1], gj[0][:, gj[1]:gj[1] + 8], AX, OP.max))
                        em.emit("vector", lambda e, j=j, gj=gj: e.tensor_reduce(
                            m1s[:, 2 * j + 1:2 * j + 2], gj[0][:, gj[1]:gj[1] + 8], AX, OP.max,
                            negate=True), inc="s_smx" if j == 3 else None)
                    em.wait("scalar", "s_smx", em.val("s_smx"))
                    for j in range(4):
                        gj = (P1, 512 + j * 8)
                        em.emit("scalar", lambda e, j=j, gj=gj: e.activation(
                            es_all[:, j * 8:(j + 1) * 8], gj[0][:, gj[1]:gj[1] + 8], ACTF.Exp,
                            bias=m1s[:, 2 * j + 1:2 * j + 2]), inc="s_smx" if j == 3 else None)
                    em.wait("vector", "s_smx", em.val("s_smx"))
                    for j in range(4):
                        em.emit("vector", lambda e, j=j: e.tensor_reduce(
                            sums8[:, j:j + 1], es_all[:, j * 8:(j + 1) * 8], AX, OP.add))
                    for j in range(4):
                        gj = (P1, 512 + j * 8)
                        em.emit("vector", lambda e, j=j, gj=gj: e.tensor_scalar(
                            pen_all[:, j * 8:(j + 1) * 8], gj[0][:, gj[1]:gj[1] + 8],
                            m1s[:, 2 * j:2 * j + 1], None, OP.is_ge))
                    for j in range(4):
                        em.emit("vector", lambda e, j=j: e.tensor_tensor(
                            pba[:, j * 8:(j + 1) * 8], pen_all[:, j * 8:(j + 1) * 8],
                            bigneg[:, :], OP.mult))
                    for j in range(4):
                        gj = (P1, 512 + j * 8)
                        em.emit("vector", lambda e, j=j, gj=gj: e.tensor_tensor(
                            msk_all[:, j * 8:(j + 1) * 8], pba[:, j * 8:(j + 1) * 8],
                            gj[0][:, gj[1]:gj[1] + 8], OP.add))
                    for j in range(4):
                        em.emit("vector", lambda e, j=j: e.tensor_reduce(
                            m2s8[:, j:j + 1], msk_all[:, j * 8:(j + 1) * 8], AX, OP.max))
                    for j in range(4):
                        gj = (P1, 512 + j * 8)
                        em.emit("vector", lambda e, j=j, gj=gj: e.tensor_tensor(
                            es2_all[:, j * 8:(j + 1) * 8], gj[0][:, gj[1]:gj[1] + 8],
                            ap3(m2s8, j, [[4, 128], [0, 8]]), OP.is_ge))
                    for j in range(4):
                        em.emit("vector", lambda e, j=j: e.tensor_tensor(
                            combtok[:, j * 8:(j + 1) * 8], es_all[:, j * 8:(j + 1) * 8],
                            es2_all[:, j * 8:(j + 1) * 8], OP.mult),
                            inc="s_smx" if j == 3 else None)
                    em.wait("tensor", "s_smx", em.val("s_smx"))
                    for j in range(4):
                        em.emit("tensor", lambda e, j=j: e.transpose(
                            P1[0:8, j * 128:(j + 1) * 128], combtok[:, j * 8:(j + 1) * 8],
                            idf_sb[:, :]))
                    for j in range(4):
                        em.emit("tensor", lambda e, j=j: e.transpose(
                            P1[0:1, 512 + j * 128: 512 + (j + 1) * 128], sums8[:, j:j + 1],
                            idf_sb[:, :]), inc="s_trb" if j == 3 else None)
                    em.wait("vector", "s_trb")
                    em.emit("vector", lambda e: e.tensor_copy(cu_sb[:, :], P1[0:8, 0:512]))
                    em.emit("vector", lambda e: e.tensor_copy(srow_sb[:, :], P1[0:1, 512:1024]),
                            inc="s_rsv")
                    em.wait("scalar", "s_rsv", em.val("s_rsv"))
                    em.emit("scalar", lambda e: e.activation(rs_sb[0:1, :], srow_sb[0:1, :], ACTF.Ln))
                    em.emit("scalar", lambda e: e.activation(
                        rs_sb[0:1, :], rs_sb[0:1, :], ACTF.Exp, scale=-1.0), inc="s_rsv")
                    em.wait("tensor", "s_rsv", em.val("s_rsv"))
                    em.emit("tensor", lambda e: e.matmul(
                        P1[0:8, 512:1024], ones1_sb[0:1, 0:8], rs_sb[0:1, :],
                        start=True, stop=True), inc="s_trb")
                    em.wait("vector", "s_trb", em.val("s_trb"))
                    em.emit("vector", lambda e, tt=tt: e.tensor_tensor(
                        comb_bf[:, tt * 512:(tt + 1) * 512], cu_sb[:, :], P1[0:8, 512:1024],
                        OP.mult), inc="s_comb")
                    if _DBG and p == 0 and tt == 0:
                        em.emit("sync", lambda e: e.dma_start(DBG_G.ap(), gsb[:, :]), inc="s_dbg", k=16)
                        em.emit("sync", lambda e: e.dma_start(DBG_CT.ap(), combtok[:, :]), inc="s_dbg", k=16)
                        em.wait("sync", "s_comb", em.val("s_comb"))
                        em.wait("sync", "s_gsb", em.val("s_gsb"))
                    if _DBG and p == 0 and tt == NTT - 1:
                        em.wait("sync", "s_comb", em.val("s_comb"))
                        em.wait("gpsimd", "s_comb", em.val("s_comb"))
                        em.emit("gpsimd", lambda e: e.dma_start(
                            ap3(DBG_CB, 0, [[NTT * 512, 8], [1, NTT * 512]]),
                            ap3(comb_bf, 0, [[NTT * 512, 8], [1, NTT * 512]])), inc="s_dbg", k=16)

                em.wait("tensor", "s_acA", em.val("s_acA"))
                for e_i in range(8):
                    em.wait("tensor", "s_comb", em.val("s_comb"))
                    for tt in range(NTT):
                        if em.val("s_bcePE") >= 1:
                            em.wait("tensor", "s_bceS", em.val("s_bcePE"))
                        em.emit("tensor", lambda e, e_i=e_i, tt=tt: e.matmul(
                            P2[:, 0:512], sel8_sb[:, e_i * 128:(e_i + 1) * 128],
                            comb_bf[:, tt * 512:(tt + 1) * 512], start=True, stop=True), inc="s_bcePE")
                        em.wait("scalar", "s_bcePE", em.val("s_bcePE"))
                        if em.val("s_bceS") >= 2:
                            em.wait("scalar", "s_y", hmm_cnt)
                        em.emit("scalar", lambda e, tt=tt: e.copy(
                            bce_sb[:, tt * 512:(tt + 1) * 512], P2[:, 0:512]), inc="s_bceS")
                    for half in range(2):
                        gh = e_i * 2 + half
                        buf = gh % 2
                        if w1_last_use[buf] > 0:
                            em.wait("gpsimd", "s_gmm", w1_last_use[buf])
                        if gh == 1:
                            em.wait("gpsimd", "s_wt", em.val("s_wt"))
                        em.emit("gpsimd", lambda e, e_i=e_i, half=half, buf=buf: e.dma_start(
                            ap3(w1h[buf], 0, [[16384, 128], [2048, NC], [1, 1024]]),
                            ap3(W1T, e_i * D * 4 * D + half * 1024,
                                [[4 * D, 128], [128 * 4 * D, NC], [1, 1024]])), inc="s_w1", k=16)
                        em.emit("gpsimd", lambda e, e_i=e_i, half=half, buf=buf: e.dma_start(
                            ap3(w1h[buf], 1024, [[16384, 128], [2048, NC], [1, 1024]]),
                            ap3(W1T, e_i * D * 4 * D + 2048 + half * 1024,
                                [[4 * D, 128], [128 * 4 * D, NC], [1, 1024]])), inc="s_w1", k=16)
                        w1_loads += 2
                        if gh >= 1 or p > 0:
                            em.wait("gpsimd", "s_w2mm", w2mm_cnt)
                        em.emit("gpsimd", lambda e, e_i=e_i, half=half: e.dma_start(
                            ap3(w2h, 0, [[8 * 1024, 128], [1024, NC], [1, 1024]]),
                            ap3(W2T, e_i * 2 * D * D + half * 1024 * D,
                                [[D, 128], [128 * D, NC], [1, 1024]])), inc="s_w2", k=16)
                        w2_loads += 1

                        em.wait("tensor", "s_w1", 16 * (w1_loads))
                        em.wait("tensor", "s_bceS", em.val("s_bceS"))
                        for pr in range(8):
                            for tt in range(NTT):
                                pi = hmm_cnt % 2
                                gb, ub = (0, 1) if pi == 0 else (2, 3)
                                if hmm_cnt >= 2:
                                    em.wait("tensor", "s_y", hmm_cnt - 1)
                                gst, gco = BANK(gb)
                                ust, uco = BANK(ub)
                                for k in range(NC):
                                    em.emit("tensor", lambda e, pr=pr, tt=tt, k=k, gst=gst, gco=gco, buf=buf: e.matmul(
                                        gst[:, gco:gco + 512],
                                        w1h[buf][:, (k * 2 + 0) * 1024 + pr * 128: (k * 2 + 0) * 1024 + (pr + 1) * 128],
                                        xnp[:, k * TP + tt * 512: k * TP + (tt + 1) * 512],
                                        start=(k == 0), stop=(k == NC - 1)))
                                for k in range(NC):
                                    em.emit("tensor", lambda e, pr=pr, tt=tt, k=k, ust=ust, uco=uco, buf=buf: e.matmul(
                                        ust[:, uco:uco + 512],
                                        w1h[buf][:, (k * 2 + 1) * 1024 + pr * 128: (k * 2 + 1) * 1024 + (pr + 1) * 128],
                                        xnp[:, k * TP + tt * 512: k * TP + (tt + 1) * 512],
                                        start=(k == 0), stop=(k == NC - 1)),
                                        inc="s_gmm" if k == NC - 1 else None)
                                hmm_cnt += 1
                                em.wait("scalar", "s_gmm", hmm_cnt)
                                em.wait("scalar", "s_y", em.val("s_y"))
                                em.emit("scalar", lambda e, gst=gst, gco=gco: e.activation(
                                    silu_sb[:, :], gst[:, gco:gco + 512], ACTF.Silu), inc="s_silu")
                                em.wait("vector", "s_silu", em.val("s_silu"))
                                em.emit("vector", lambda e, ust=ust, uco=uco: e.tensor_tensor(
                                    tmf[:, :], silu_sb[:, :], ust[:, uco:uco + 512], OP.mult))
                                em.emit("vector", lambda e, pr=pr, tt=tt: e.tensor_tensor(
                                    y_sb[:, (pr * NTT + tt) * 512: (pr * NTT + tt + 1) * 512],
                                    tmf[:, :], bce_sb[:, tt * 512:(tt + 1) * 512], OP.mult), inc="s_y")
                        w1_last_use[buf] = em.val("s_gmm")
                        em.wait("tensor", "s_y", em.val("s_y"))
                        em.wait("tensor", "s_w2", 16 * w2_loads)
                        for tt in range(NTT):
                            for m in range(NC):
                                wb = 6 + w2mm_cnt % 2
                                if w2mm_cnt >= 2:
                                    em.wait("tensor", "s_acF", w2mm_cnt - 1)
                                pst, pco = BANK(wb)
                                for k in range(8):
                                    em.emit("tensor", lambda e, tt=tt, m=m, k=k, pst=pst, pco=pco: e.matmul(
                                        pst[:, pco:pco + 512],
                                        w2h[:, k * 1024 + m * 128: k * 1024 + (m + 1) * 128],
                                        y_sb[:, (k * NTT + tt) * 512: (k * NTT + tt + 1) * 512],
                                        start=(k == 0), stop=(k == 7)),
                                        inc="s_w2mm" if k == 7 else None)
                                w2mm_cnt += 1
                                em.wait("vector", "s_w2mm", w2mm_cnt)
                                em.emit("vector", lambda e, tt=tt, m=m, pst=pst, pco=pco: e.tensor_tensor(
                                    accb[:, m * TP + tt * 512: m * TP + (tt + 1) * 512],
                                    accb[:, m * TP + tt * 512: m * TP + (tt + 1) * 512],
                                    pst[:, pco:pco + 512], OP.add), inc="s_acF")

                em.wait("sync", "s_acF", em.val("s_acF"))
                em.emit("sync", lambda e, p=p: e.dma_start(
                    ap3(OUT, p * TP, [[TOWN, 128], [128 * TOWN, NC], [1, TP]]),
                    ap3(accb, 0, [[NC * TP, 128], [TP, NC], [1, TP]])), inc="s_out", k=16)
                if p < NPASS - 1:
                    em.wait("gpsimd", "s_gmm", em.val("s_gmm"))
                    load_wqwo()
                em.nop_inc("tensor", "sPB")
                em.nop_inc("vector", "sVB")

            if _DBG:
                em.wait("vector", "s_smx", 3)
                em.emit("sync", lambda e: e.dma_start(DBG_ES.ap(), dbges[:, :]), inc="s_dbg", k=16)
                em.emit("sync", lambda e: e.dma_start(DBG_SUM.ap(), dbgsum[:, :]), inc="s_dbg", k=16)
            em.wait("sync", "s_out", 16 * NPASS)
            if _DBG:
                em.wait("sync", "s_dbg", em.val("s_dbg"))

            with nc.Block() as blk:
                em.replay(blk)

    return nc



def _tables():
    invf = (1.0 / (500000.0 ** (np.arange(0, HD, 2, dtype=np.float64) / HD))).astype(np.float64)
    t = np.arange(T, dtype=np.float64)
    ang = t[:, None] * invf[None, :]
    emb = np.concatenate([ang, ang], axis=1)
    return np.cos(emb).astype(np.float32), np.sin(emb).astype(np.float32)


def _consts():
    sel16 = np.zeros((16, NC * 128), dtype=ml_dtypes.bfloat16)
    for c in range(NC):
        for m in range(128):
            sel16[2 * c + m // 64, c * 128 + m] = 1.0
    sel8 = np.zeros((8, 8 * 128), dtype=ml_dtypes.bfloat16)
    for e in range(8):
        sel8[e, e * 128:(e + 1) * 128] = 1.0
    ones1 = np.ones((1, 128), dtype=np.float32)
    onesb = np.ones((128, 1), dtype=ml_dtypes.bfloat16)
    idf = np.eye(128, dtype=np.float32)
    epsv = np.full((128, 1), 1e-5, dtype=np.float32)
    return sel16, sel8, ones1, onesb, idf, epsv


def kernel(L0_attn_x, L0_attn_W_qkv, L0_attn_W_out, L0_ffn_w1, L0_ffn_w2, L0_ffn_router_w):
    x = np.asarray(L0_attn_x, dtype=np.float32)
    wqkv = np.ascontiguousarray(np.asarray(L0_attn_W_qkv, dtype=np.float32))
    wout = np.ascontiguousarray(np.asarray(L0_attn_W_out, dtype=np.float32))
    w1 = np.asarray(L0_ffn_w1, dtype=np.float32)
    w2 = np.asarray(L0_ffn_w2, dtype=np.float32)
    wr = np.asarray(L0_ffn_router_w, dtype=np.float32)

    w1t = np.ascontiguousarray(w1.transpose(0, 2, 1))
    w2t = np.ascontiguousarray(w2.transpose(0, 2, 1))
    wrt = np.ascontiguousarray(wr.T)

    cos64, sin64 = _tables()
    sel16, sel8, ones1, onesb, idf, epsv = _consts()

    in_maps = []
    for c in range(8):
        b, h = c // 2, c % 2
        xT = np.ascontiguousarray(x[b].T)
        own = xT[:, h * TOWN:(h + 1) * TOWN]
        oth = xT[:, (1 - h) * TOWN:(2 - h) * TOWN]
        x_core = np.ascontiguousarray(np.concatenate([own, oth], axis=1))
        order = np.concatenate([np.arange(h * TOWN, (h + 1) * TOWN),
                                np.arange((1 - h) * TOWN, (2 - h) * TOWN)])
        cost = np.ascontiguousarray(cos64[order])
        sint = np.ascontiguousarray(sin64[order])
        cf = cos64[h * TOWN:(h + 1) * TOWN, :].T
        sf = sin64[h * TOWN:(h + 1) * TOWN, :].T
        cosf = np.ascontiguousarray(np.concatenate([cf, cf], axis=0))
        sinf = np.ascontiguousarray(np.concatenate([sf, sf], axis=0))
        in_maps.append(dict(
            x=x_core, wqkv=wqkv, wout=wout, w1t=w1t, w2t=w2t, wrt=wrt,
            cost=cost, sint=sint, cosf=cosf, sinf=sinf,
            sel16=sel16, sel8=sel8, ones1=ones1, onesb=onesb, idf=idf, epsv=epsv))

    nc = build_program()
    res = run_bass_kernel_spmd(nc, in_maps, core_ids=list(range(8)))

    out = np.empty((4, T, D), dtype=np.float32)
    for c in range(8):
        b, h = c // 2, c % 2
        out[b, h * TOWN:(h + 1) * TOWN, :] = res.results[c]["out"].T
    return out


# version 14
# speedup vs baseline: 1.0304x; 1.0304x over previous
from contextlib import ExitStack
import numpy as np
import ml_dtypes
import concourse.bass as bass
import concourse.mybir as mybir
from concourse.alu_op_type import AluOpType as OP
from concourse.bass_utils import run_bass_kernel_spmd

F32 = mybir.dt.float32
BF16 = mybir.dt.bfloat16
ACTF = mybir.ActivationFunctionType
AX = mybir.AxisListType.X

D = 1024
T = 8192
TOWN = 4096
NC = 8
H = 16
HD = 64
TILE_A = 512
NT_A = T // TILE_A
TP = 1024
NPASS = TOWN // TP
TT = 512
NTT = TP // TT
BIG = 30000.0

ENGS = ("sync", "gpsimd", "tensor", "vector", "scalar")

_DBG = False


class Em:

    def __init__(self, nc, ctx):
        self.nc = nc
        self.ctx = ctx
        self.ops = {e: [] for e in ENGS}
        self.sem = {}
        self.cnt = {}

    def new_sem(self, name):
        self.sem[name] = self.ctx.enter_context(self.nc.semaphore(name))
        self.cnt[name] = 0

    def emit(self, eng, fn, inc=None, k=1):
        if inc is None:
            self.ops[eng].append(fn)
        else:
            h = self.sem[inc]
            self.cnt[inc] += k
            self.ops[eng].append(lambda e, fn=fn, h=h, k=k: fn(e).then_inc(h, k))

    def wait(self, eng, name, val=None):
        v = self.cnt[name] if val is None else val
        if v <= 0:
            return
        assert v <= self.cnt[name], f"wait {name} {v} > emitted {self.cnt[name]}"
        h = self.sem[name]
        self.ops[eng].append(lambda e, h=h, v=v: e.wait_ge(h, v))

    def val(self, name):
        return self.cnt[name]

    def dma(self, eng, name, fn):
        self.wait(eng, name)
        self.emit(eng, fn, inc=name, k=16)

    def nop_inc(self, eng, name, k=1):
        h = self.sem[name]
        self.cnt[name] += k
        self.ops[eng].append(lambda e, h=h, k=k: e.drain().then_inc(h, k))

    def replay(self, blk):
        for eng in ENGS:
            dec = getattr(blk, eng)
            ops = self.ops[eng]

            def body(engine, ops=ops):
                for op in ops:
                    op(engine)
            dec(body)
        self.ops = {e: [] for e in ENGS}


def ap3(t, off, dims):
    return bass.AP(t, off, dims)


def build_program():
    nc = bass.Bass("TRN2", target_bir_lowering=False, debug=False, detect_race_conditions=False)

    X = nc.dram_tensor("x", [D, T], F32, kind="ExternalInput")
    WQKV = nc.dram_tensor("wqkv", [D, 3 * D], F32, kind="ExternalInput")
    WOUT = nc.dram_tensor("wout", [D, D], F32, kind="ExternalInput")
    W1T = nc.dram_tensor("w1t", [8, D, 4 * D], F32, kind="ExternalInput")
    W2T = nc.dram_tensor("w2t", [8, 2 * D, D], F32, kind="ExternalInput")
    WRT = nc.dram_tensor("wrt", [D, 8], F32, kind="ExternalInput")
    COST = nc.dram_tensor("cost", [T, HD], F32, kind="ExternalInput")
    SINT = nc.dram_tensor("sint", [T, HD], F32, kind="ExternalInput")
    COSF = nc.dram_tensor("cosf", [128, TOWN], F32, kind="ExternalInput")
    SINF = nc.dram_tensor("sinf", [128, TOWN], F32, kind="ExternalInput")
    SEL16 = nc.dram_tensor("sel16", [16, NC * 128], BF16, kind="ExternalInput")
    SEL8 = nc.dram_tensor("sel8", [8, 8 * 128], BF16, kind="ExternalInput")
    ONES1 = nc.dram_tensor("ones1", [1, 128], F32, kind="ExternalInput")
    ONESB = nc.dram_tensor("onesb", [128, 1], BF16, kind="ExternalInput")
    IDF = nc.dram_tensor("idf", [128, 128], F32, kind="ExternalInput")
    EPSV = nc.dram_tensor("epsv", [128, 1], F32, kind="ExternalInput")
    XNS = nc.dram_tensor("xns", [D, TOWN], BF16, kind="Internal")
    OUT = nc.dram_tensor("out", [D, TOWN], F32, kind="ExternalOutput")
    if _DBG:
        DBG_G = nc.dram_tensor("dbg_g", [8, 512], F32, kind="ExternalOutput")
        DBG_CT = nc.dram_tensor("dbg_ct", [128, 32], F32, kind="ExternalOutput")
        DBG_CB = nc.dram_tensor("dbg_cb", [8, NTT * 512], F32, kind="ExternalOutput")
        DBG_ES = nc.dram_tensor("dbg_es", [128, 32], F32, kind="ExternalOutput")
        DBG_SUM = nc.dram_tensor("dbg_sum", [128, 16], F32, kind="ExternalOutput")

    with ExitStack() as ctx0:
        em = Em(nc, ctx0)
        for s in ("s_x0", "s_x1", "s_xb0", "s_xb1", "s_wqkv",
                  "s_ct0", "s_ct1", "s_st0", "s_st1", "s_xns0", "s_xns1", "s_cst",
                  "s_sq", "s_mmst", "s_vst", "s_rstd", "s_mmbc", "s_xn",
                  "s_mmkp", "s_evkp", "s_krV", "s_eS", "s_kfV", "s_mmkv",
                  "sP", "sV", "sS", "s_kvfin",
                  "s_xnp", "s_accx", "s_cfc", "s_cfs", "s_wq", "s_wo", "s_wr",
                  "s_w1g0", "s_w1g1", "s_w1u0", "s_w1u1", "s_w2",
                  "s_qmm", "s_qf", "s_eB", "s_krB", "s_den", "s_rd", "s_bc", "s_bcev", "s_qs",
                  "s_att", "s_attev", "s_wt", "s_acA",
                  "s_rt", "s_gsb", "s_trf", "s_smx", "s_trb", "s_comb",
                  "s_bcePE", "s_bceS", "s_gmm", "s_silu", "s_y", "s_w2mm", "s_acF",
                  "s_out", "sPB", "sVB", "s_dbg", "s_rsv"):
            em.new_sem(s)

        kvbd = ctx0.enter_context(nc.sbuf_tensor("kvbd", [128, 8 * 128], BF16))
        ksbd = ctx0.enter_context(nc.sbuf_tensor("ksbd", [128, 8 * 16], BF16))
        sel16_sb = ctx0.enter_context(nc.sbuf_tensor("sel16_sb", [16, NC * 128], BF16))
        sel8_sb = ctx0.enter_context(nc.sbuf_tensor("sel8_sb", [8, 8 * 128], BF16))
        ones1_sb = ctx0.enter_context(nc.sbuf_tensor("ones1_sb", [1, 128], F32))
        onesb_sb = ctx0.enter_context(nc.sbuf_tensor("onesb_sb", [128, 1], BF16))
        idf_sb = ctx0.enter_context(nc.sbuf_tensor("idf_sb", [128, 128], F32))
        epsv_sb = ctx0.enter_context(nc.sbuf_tensor("epsv_sb", [128, 1], F32))

        def load_consts(e):
            pass
        for dst, src in [(sel16_sb[:, :], SEL16.ap()), (sel8_sb[:, :], SEL8.ap()),
                         (ones1_sb[:, :], ONES1.ap()), (onesb_sb[:, :], ONESB.ap()),
                         (idf_sb[:, :], IDF.ap()), (epsv_sb[:, :], EPSV.ap())]:
            em.dma("sync", "s_cst", lambda e, d=dst, s=src: e.dma_start(d, s))

        with ExitStack() as ctxA:
            xf = [ctxA.enter_context(nc.sbuf_tensor(f"xf{j}", [128, NC * TILE_A], F32)) for j in range(2)]
            xbf = [ctxA.enter_context(nc.sbuf_tensor(f"xbf{j}", [128, NC * TILE_A], BF16)) for j in range(2)]
            sq_sb = ctxA.enter_context(nc.sbuf_tensor("sq_sb", [128, NC * TILE_A], BF16))
            xn_t = [ctxA.enter_context(nc.sbuf_tensor(f"xnt{j}", [128, NC * TILE_A], BF16)) for j in range(2)]
            ct_sb = [ctxA.enter_context(nc.sbuf_tensor(f"ct{j}", [128, 4 * HD], F32)) for j in range(2)]
            st_sb = [ctxA.enter_context(nc.sbuf_tensor(f"st{j}", [128, 4 * HD], F32)) for j in range(2)]
            wqkv_sb = ctxA.enter_context(nc.sbuf_tensor("wqkv_sb", [128, NC * 3 * D], BF16))
            k_tok = ctxA.enter_context(nc.sbuf_tensor("k_tok", [128, 2 * 1024], BF16))
            v_tok = ctxA.enter_context(nc.sbuf_tensor("v_tok", [128, 2 * 1024], BF16))
            rotA = ctxA.enter_context(nc.sbuf_tensor("rotA", [128, 512], F32))
            t1A = ctxA.enter_context(nc.sbuf_tensor("t1A", [128, 512], F32))
            t2A = ctxA.enter_context(nc.sbuf_tensor("t2A", [128, 512], F32))
            rAs = ctxA.enter_context(nc.sbuf_tensor("rAs", [128, 512], F32))
            eAs = ctxA.enter_context(nc.sbuf_tensor("eAs", [128, 512], F32))
            mean_sb = ctxA.enter_context(nc.sbuf_tensor("mean_sb", [1, 512], F32))
            ms_sb = ctxA.enter_context(nc.sbuf_tensor("ms_sb", [1, 512], F32))
            var_sb = ctxA.enter_context(nc.sbuf_tensor("var_sb", [1, 512], F32))
            rstd_sb = ctxA.enter_context(nc.sbuf_tensor("rstd_sb", [1, 512], F32))

            kvacc = ctxA.enter_context(nc.psum_tensor("kvacc", [128, 1024], F32))
            ksum = ctxA.enter_context(nc.psum_tensor("ksum", [128, 512], F32))
            stats = ctxA.enter_context(nc.psum_tensor("stats", [128, 512], F32))
            meanbc = ctxA.enter_context(nc.psum_tensor("meanbc", [128, 512], F32))
            rstdbc = ctxA.enter_context(nc.psum_tensor("rstdbc", [128, 512], F32))
            kprod = ctxA.enter_context(nc.psum_tensor("kprod", [128, 1024], F32))

            em.dma("gpsimd", "s_wqkv", lambda e: e.dma_start(
                ap3(wqkv_sb, 0, [[NC * 3 * D, 128], [3 * D, NC], [1, 3 * D]]),
                ap3(WQKV, 0, [[3 * D, 128], [128 * 3 * D, NC], [1, 3 * D]])))

            em.wait("tensor", "s_cst", 16 * 6)
            em.wait("scalar", "s_cst", 16 * 6)
            kp_g = 0
            kf_i = 0
            cg = 0
            nxns = 0

            for i in range(NT_A):
                b = i % 2
                own = i < NT_A // 2
                if i >= 2:
                    for s in ("sP", "sV", "sS"):
                        em.wait("sync", s, i - 1)
                        em.wait("gpsimd", s, i - 1)
                em.dma("sync", f"s_x{b}", lambda e, i=i, b=b: e.dma_start(
                    ap3(xf[b], 0, [[NC * TILE_A, 128], [TILE_A, NC], [1, TILE_A]]),
                    ap3(X, i * TILE_A, [[T, 128], [128 * T, NC], [1, TILE_A]])))
                em.dma("sync", f"s_ct{b}", lambda e, i=i, b=b: e.dma_start(
                    ap3(ct_sb[b], 0, [[4 * HD, 128], [HD, 4], [1, HD]]),
                    ap3(COST, i * TILE_A * HD, [[HD, 128], [128 * HD, 4], [1, HD]])))
                em.dma("sync", f"s_st{b}", lambda e, i=i, b=b: e.dma_start(
                    ap3(st_sb[b], 0, [[4 * HD, 128], [HD, 4], [1, HD]]),
                    ap3(SINT, i * TILE_A * HD, [[HD, 128], [128 * HD, 4], [1, HD]])))
                em.dma("gpsimd", f"s_xb{b}", lambda e, i=i, b=b: e.dma_start(
                    ap3(xbf[b], 0, [[NC * TILE_A, 128], [TILE_A, NC], [1, TILE_A]]),
                    ap3(X, i * TILE_A, [[T, 128], [128 * T, NC], [1, TILE_A]])))

                em.wait("scalar", f"s_x{b}")
                if i >= 1:
                    em.wait("scalar", "sP", i)
                for c in range(NC):
                    em.emit("scalar", lambda e, b=b, c=c: e.activation(
                        sq_sb[:, c * 512:(c + 1) * 512], xf[b][:, c * 512:(c + 1) * 512], ACTF.Square),
                        inc="s_sq" if c == NC - 1 else None)

                em.wait("tensor", f"s_xb{b}")
                em.wait("tensor", "s_wqkv")
                em.wait("tensor", "s_sq")
                if i >= 1:
                    em.wait("tensor", "sV", i)
                for c in range(NC):
                    em.emit("tensor", lambda e, b=b, c=c: e.matmul(
                        stats[0:1, :], onesb_sb[:, :], xbf[b][:, c * 512:(c + 1) * 512],
                        start=(c == 0), stop=(c == NC - 1)))
                for c in range(NC):
                    em.emit("tensor", lambda e, c=c: e.matmul(
                        stats[32:33, :], onesb_sb[:, :], sq_sb[:, c * 512:(c + 1) * 512],
                        start=(c == 0), stop=(c == NC - 1)),
                        inc="s_mmst" if c == NC - 1 else None)

                em.wait("vector", "s_mmst")
                em.emit("vector", lambda e: e.tensor_scalar(mean_sb[:, :], stats[0:1, :], 1.0 / D, None, OP.mult))
                em.emit("vector", lambda e: e.tensor_tensor(ms_sb[:, :], mean_sb[:, :], mean_sb[:, :], OP.mult))
                em.emit("vector", lambda e: e.scalar_tensor_tensor(
                    var_sb[:, :], stats[32:33, :], 1.0 / D, ms_sb[:, :], OP.mult, OP.subtract), inc="s_vst")
                em.wait("scalar", "s_vst")
                em.emit("scalar", lambda e: e.activation(rstd_sb[:, :], var_sb[:, :], ACTF.Ln, bias=epsv_sb[0:1, 0:1]))
                em.emit("scalar", lambda e: e.activation(rstd_sb[:, :], rstd_sb[:, :], ACTF.Exp, scale=-0.5), inc="s_rstd")

                em.wait("tensor", "s_rstd")
                em.wait("tensor", "s_cst", 16 * 6)
                em.emit("tensor", lambda e: e.matmul(meanbc[:, :], ones1_sb[:, :], mean_sb[:, :], start=True, stop=True))
                em.emit("tensor", lambda e: e.matmul(rstdbc[:, :], ones1_sb[:, :], rstd_sb[:, :], start=True, stop=True),
                        inc="s_mmbc")

                em.wait("vector", "s_mmbc")
                if i >= 2:
                    em.wait("vector", "sP", i - 1)
                    if i - 2 < 8:
                        em.wait("vector", f"s_xns{b}", 16 * ((i - 2) // 2 + 1))
                    else:
                        em.wait("vector", f"s_xns{b}", 16 * 4)
                for c in range(NC):
                    em.emit("vector", lambda e, b=b, c=c: e.tensor_tensor(
                        t1A[:, :], xf[b][:, c * 512:(c + 1) * 512], meanbc[:, :], OP.subtract))
                    em.emit("vector", lambda e, b=b, c=c: e.tensor_tensor(
                        xn_t[b][:, c * 512:(c + 1) * 512], t1A[:, :], rstdbc[:, :], OP.mult),
                        inc="s_xn" if c == NC - 1 else None)

                if own:
                    em.wait("sync", "s_xn", em.val("s_xn"))
                    em.dma("sync", f"s_xns{b}", lambda e, i=i, b=b: e.dma_start(
                        ap3(XNS, i * TILE_A, [[TOWN, 128], [128 * TOWN, NC], [1, TILE_A]]),
                        ap3(xn_t[b], 0, [[NC * TILE_A, 128], [TILE_A, NC], [1, TILE_A]])))
                    nxns += 1

                em.wait("tensor", "s_xn")
                em.wait("tensor", f"s_ct{b}")
                em.wait("tensor", f"s_st{b}")
                for cc in range(4):
                    kb = cc % 2
                    for f in range(4):
                        bank = kp_g % 2
                        if kp_g >= 2:
                            em.wait("tensor", "s_evkp", kp_g - 1)
                        for c in range(NC):
                            em.emit("tensor", lambda e, b=b, c=c, cc=cc, f=f, bank=bank: e.matmul(
                                kprod[:, bank * 512:(bank + 1) * 512],
                                xn_t[b][:, c * 512 + cc * 128: c * 512 + cc * 128 + 128],
                                wqkv_sb[:, c * 3 * D + D + f * 512: c * 3 * D + D + (f + 1) * 512],
                                start=(c == 0), stop=(c == NC - 1)),
                                inc="s_mmkp" if c == NC - 1 else None)
                        if f >= 2:
                            em.wait("scalar", "s_mmkp", kp_g + 1)
                            if cg >= 2:
                                em.wait("scalar", "s_mmkv", cg - 1)
                            em.emit("scalar", lambda e, kb=kb, f=f, bank=bank: e.copy(
                                v_tok[:, kb * 1024 + (f - 2) * 512: kb * 1024 + (f - 1) * 512],
                                kprod[:, bank * 512:(bank + 1) * 512]), inc="s_evkp")
                        else:
                            em.wait("vector", "s_mmkp", kp_g + 1)
                            if kf_i >= 1:
                                em.wait("vector", "s_eS", kf_i)
                            em.emit("vector", lambda e, bank=bank: e.tensor_scalar(
                                ap3(rotA, 0, [[512, 128], [64, 8], [1, 32]]),
                                ap3(kprod, bank * 512 + 32, [[1024, 128], [64, 8], [1, 32]]),
                                -1.0, None, OP.mult))
                            em.emit("vector", lambda e, bank=bank: e.tensor_copy(
                                ap3(rotA, 32, [[512, 128], [64, 8], [1, 32]]),
                                ap3(kprod, bank * 512, [[1024, 128], [64, 8], [1, 32]])))
                            em.emit("vector", lambda e, b=b, cc=cc, bank=bank: e.tensor_tensor(
                                ap3(t1A, 0, [[512, 128], [64, 8], [1, 64]]),
                                ap3(kprod, bank * 512, [[1024, 128], [64, 8], [1, 64]]),
                                ap3(ct_sb[b], cc * HD, [[4 * HD, 128], [0, 8], [1, 64]]),
                                OP.mult), inc="s_evkp")
                            em.emit("vector", lambda e, b=b, cc=cc: e.tensor_tensor(
                                ap3(t2A, 0, [[512, 128], [64, 8], [1, 64]]),
                                ap3(rotA, 0, [[512, 128], [64, 8], [1, 64]]),
                                ap3(st_sb[b], cc * HD, [[4 * HD, 128], [0, 8], [1, 64]]),
                                OP.mult))
                            em.emit("vector", lambda e: e.tensor_tensor(t1A[:, :], t1A[:, :], t2A[:, :], OP.add))
                            em.emit("vector", lambda e: e.tensor_scalar(rAs[:, :], t1A[:, :], 0.0, None, OP.max),
                                    inc="s_krV")
                            em.wait("scalar", "s_krV")
                            if kf_i >= 1:
                                em.wait("scalar", "s_kfV", kf_i)
                            em.emit("scalar", lambda e: e.activation(eAs[:, :], t1A[:, :], ACTF.Exp), inc="s_eS")
                            em.wait("vector", "s_eS")
                            if cg >= 2:
                                em.wait("vector", "s_mmkv", cg - 1)
                            em.emit("vector", lambda e, kb=kb, f=f: e.scalar_tensor_tensor(
                                k_tok[:, kb * 1024 + f * 512: kb * 1024 + (f + 1) * 512],
                                eAs[:, :], 1.0, rAs[:, :], OP.min, OP.add), inc="s_kfV")
                            kf_i += 1
                        kp_g += 1
                    em.wait("tensor", "s_kfV", 2 * (cg + 1))
                    em.wait("tensor", "s_evkp", em.val("s_evkp"))
                    last_mm_of_tile = (cc == 3)
                    for p in range(8):
                        st_kv = (cg == 0 and p in (0, 4))
                        st_ks = (cg == 0 and p == 0)
                        fin = (cg == 63)
                        em.emit("tensor", lambda e, kb=kb, p=p, st=st_kv, fin=fin: e.matmul(
                            kvacc[:, p * 128:(p + 1) * 128],
                            k_tok[:, kb * 1024 + p * 128: kb * 1024 + (p + 1) * 128],
                            v_tok[:, kb * 1024 + p * 128: kb * 1024 + (p + 1) * 128],
                            start=st, stop=fin, skip_group_check=True))
                        em.emit("tensor", lambda e, kb=kb, p=p, st=st_ks, fin=fin: e.matmul(
                            ksum[:, p:p + 1],
                            k_tok[:, kb * 1024 + p * 128: kb * 1024 + (p + 1) * 128],
                            onesb_sb[:, :],
                            start=st, stop=(fin and p == 7), skip_group_check=True),
                            inc="s_mmkv" if p == 7 else None)
                    cg += 1
                em.nop_inc("tensor", "sP")
                em.nop_inc("vector", "sV")
                em.nop_inc("scalar", "sS")

            em.wait("vector", "sP", NT_A)
            em.emit("vector", lambda e: e.memset(kvbd[:, :], 0.0))
            em.emit("vector", lambda e: e.memset(ksbd[:, :], 0.0))
            for p in range(8):
                em.emit("vector", lambda e, p=p: e.tensor_copy(
                    kvbd[0:64, p * 128: p * 128 + 64], kvacc[0:64, p * 128: p * 128 + 64]))
                em.emit("vector", lambda e, p=p: e.tensor_copy(
                    kvbd[64:128, p * 128 + 64: p * 128 + 128], kvacc[64:128, p * 128 + 64: p * 128 + 128]))
                em.emit("vector", lambda e, p=p: e.tensor_copy(
                    ksbd[0:64, p * 16 + 2 * p: p * 16 + 2 * p + 1], ksum[0:64, p:p + 1]))
                em.emit("vector", lambda e, p=p: e.tensor_copy(
                    ksbd[64:128, p * 16 + 2 * p + 1: p * 16 + 2 * p + 2], ksum[64:128, p:p + 1]),
                    inc="s_kvfin" if p == 7 else None)
            em.wait("sync", "s_xns0", 16 * 4)
            em.wait("sync", "s_xns1", 16 * 4)
            em.wait("sync", "s_x0")
            em.wait("sync", "s_x1")

            with nc.Block() as blk:
                em.replay(blk)

        with ExitStack() as ctxB:
            xnp = ctxB.enter_context(nc.sbuf_tensor("xnp", [128, NC * TP], BF16))
            accb = ctxB.enter_context(nc.sbuf_tensor("accb", [128, NC * TP], F32))
            w1h = [ctxB.enter_context(nc.sbuf_tensor(f"w1h{j}", [128, 16384], BF16)) for j in range(2)]
            w2h = ctxB.enter_context(nc.sbuf_tensor("w2h", [128, 8 * 1024], BF16))
            y_sb = ctxB.enter_context(nc.sbuf_tensor("y_sb", [128, 8 * NTT * 512], BF16))
            qf_bf = ctxB.enter_context(nc.sbuf_tensor("qf_bf", [128, NC * 512], BF16))
            rotB = ctxB.enter_context(nc.sbuf_tensor("rotB", [128, 512], F32))
            t1B = ctxB.enter_context(nc.sbuf_tensor("t1B", [128, 512], F32))
            t2B = ctxB.enter_context(nc.sbuf_tensor("t2B", [128, 512], F32))
            rBs = ctxB.enter_context(nc.sbuf_tensor("rBs", [128, 512], F32))
            eBs = ctxB.enter_context(nc.sbuf_tensor("eBs", [128, 512], F32))
            wr_sb = ctxB.enter_context(nc.sbuf_tensor("wr_sb", [128, NC * 8], BF16))
            cfp = ctxB.enter_context(nc.sbuf_tensor("cfp", [128, TP], F32))
            sfp = ctxB.enter_context(nc.sbuf_tensor("sfp", [128, TP], F32))
            rd_f = ctxB.enter_context(nc.sbuf_tensor("rd_f", [16, 512], F32))
            rd_bf = ctxB.enter_context(nc.sbuf_tensor("rd_bf", [16, 512], BF16))
            bcq_bf = ctxB.enter_context(nc.sbuf_tensor("bcq_bf", [128, 512], BF16))
            gsb = ctxB.enter_context(nc.sbuf_tensor("gsb", [8, 512], F32))
            m1_c = ctxB.enter_context(nc.sbuf_tensor("m1_c", [128, 8], F32))
            pen_c = ctxB.enter_context(nc.sbuf_tensor("pen_c", [128, 8], F32))
            msk_c = ctxB.enter_context(nc.sbuf_tensor("msk_c", [128, 8], F32))
            es_c = ctxB.enter_context(nc.sbuf_tensor("es_c", [128, 8], F32))
            es2_c = ctxB.enter_context(nc.sbuf_tensor("es2_c", [128, 8], F32))
            es_all = ctxB.enter_context(nc.sbuf_tensor("es_all", [128, 32], F32))
            m1s = ctxB.enter_context(nc.sbuf_tensor("m1s", [128, 8], F32))
            sums8 = ctxB.enter_context(nc.sbuf_tensor("sums8", [128, 8], F32))
            cu_sb = ctxB.enter_context(nc.sbuf_tensor("cu_sb", [8, 512], F32))
            pen_all = ctxB.enter_context(nc.sbuf_tensor("pen_all", [128, 32], F32))
            pba = ctxB.enter_context(nc.sbuf_tensor("pba", [128, 32], F32))
            msk_all = ctxB.enter_context(nc.sbuf_tensor("msk_all", [128, 32], F32))
            m2s8 = ctxB.enter_context(nc.sbuf_tensor("m2s8", [128, 4], F32))
            es2_all = ctxB.enter_context(nc.sbuf_tensor("es2_all", [128, 32], F32))
            bigneg = ctxB.enter_context(nc.sbuf_tensor("bigneg", [128, 8], F32))
            srow_sb = ctxB.enter_context(nc.sbuf_tensor("srow_sb", [1, 512], F32))
            rs_sb = ctxB.enter_context(nc.sbuf_tensor("rs_sb", [1, 512], F32))
            sum_c = ctxB.enter_context(nc.sbuf_tensor("sum_c", [128, 4], F32))
            combtok = ctxB.enter_context(nc.sbuf_tensor("combtok", [128, 32], F32))
            dbges = ctxB.enter_context(nc.sbuf_tensor("dbges", [128, 32], F32))
            dbgsum = ctxB.enter_context(nc.sbuf_tensor("dbgsum", [128, 16], F32))
            comb_bf = ctxB.enter_context(nc.sbuf_tensor("comb_bf", [8, NTT * 512], BF16))
            bce_sb = ctxB.enter_context(nc.sbuf_tensor("bce_sb", [128, NTT * 512], F32))
            silu_sb = ctxB.enter_context(nc.sbuf_tensor("silu_sb", [128, 512], F32))
            tmf = ctxB.enter_context(nc.sbuf_tensor("tmf", [128, 512], F32))

            P0 = ctxB.enter_context(nc.psum_tensor("P0", [128, 1024], F32))
            P1 = ctxB.enter_context(nc.psum_tensor("P1", [128, 1024], F32))
            P2 = ctxB.enter_context(nc.psum_tensor("P2", [128, 1024], F32))
            P3 = ctxB.enter_context(nc.psum_tensor("P3", [128, 1024], F32))

            def BANK(k):
                return ((P0, P1, P2, P3)[k // 2], (k % 2) * 512)

            wq_sb = w1h[1]
            WQ_OFF, WO_OFF = 0, 8192

            w1_loads = 0
            w1_last_use = {0: 0, 1: 0}
            w2_loads = 0
            hmm_cnt = 0
            w2mm_cnt = 0

            def load_wqwo():
                em.dma("gpsimd", "s_wq", lambda e: e.dma_start(
                    ap3(wq_sb, WQ_OFF, [[16384, 128], [1024, NC], [1, 1024]]),
                    ap3(WQKV, 0, [[3 * D, 128], [128 * 3 * D, NC], [1, 1024]])))
                em.dma("gpsimd", "s_wo", lambda e: e.dma_start(
                    ap3(wq_sb, WO_OFF, [[16384, 128], [1024, NC], [1, 1024]]),
                    ap3(WOUT, 0, [[D, 128], [128 * D, NC], [1, 1024]])))

            em.emit("vector", lambda e: e.memset(bigneg[:, :], -BIG))
            em.dma("gpsimd", "s_wr", lambda e: e.dma_start(
                ap3(wr_sb, 0, [[NC * 8, 128], [8, NC], [1, 8]]),
                ap3(WRT, 0, [[8, 128], [128 * 8, NC], [1, 8]])))
            load_wqwo()

            for p in range(NPASS):
                if p >= 1:
                    em.wait("sync", "sPB", p)
                    em.wait("sync", "sVB", p)
                    em.wait("sync", "s_out", 16 * p)
                em.dma("sync", "s_xnp", lambda e, p=p: e.dma_start(
                    ap3(xnp, 0, [[NC * TP, 128], [TP, NC], [1, TP]]),
                    ap3(XNS, p * TP, [[TOWN, 128], [128 * TOWN, NC], [1, TP]])))
                em.dma("sync", "s_accx", lambda e, p=p: e.dma_start(
                    ap3(accb, 0, [[NC * TP, 128], [TP, NC], [1, TP]]),
                    ap3(X, p * TP, [[T, 128], [128 * T, NC], [1, TP]])))
                em.dma("sync", "s_cfc", lambda e, p=p: e.dma_start(
                    cfp[:, :], ap3(COSF, p * TP, [[TOWN, 128], [1, TP]])))
                em.dma("sync", "s_cfs", lambda e, p=p: e.dma_start(
                    sfp[:, :], ap3(SINF, p * TP, [[TOWN, 128], [1, TP]])))

                em.wait("tensor", "s_kvfin", 1)
                em.wait("tensor", "s_y", em.val("s_y"))
                em.wait("tensor", "s_acF", em.val("s_acF"))
                em.wait("tensor", "s_bceS", em.val("s_bceS"))
                for tt in range(NTT):
                    em.wait("tensor", "s_xnp")
                    em.wait("tensor", "s_wq")
                    for c in range(NC):
                        qb = c % 2
                        if em.val("s_qmm") >= 2:
                            em.wait("tensor", "s_qf", em.val("s_qmm") - 1)
                        for k in range(NC):
                            em.emit("tensor", lambda e, c=c, k=k, tt=tt, qb=qb: e.matmul(
                                BANK(qb)[0][:, BANK(qb)[1]:BANK(qb)[1] + 512],
                                wq_sb[:, WQ_OFF + k * 1024 + c * 128: WQ_OFF + k * 1024 + (c + 1) * 128],
                                xnp[:, k * TP + tt * 512: k * TP + (tt + 1) * 512],
                                start=(k == 0), stop=(k == NC - 1)),
                                inc="s_qmm" if k == NC - 1 else None)
                        em.wait("vector", "s_qmm")
                        em.wait("vector", "s_cfc")
                        em.wait("vector", "s_cfs")
                        if em.val("s_qf") >= 1:
                            em.wait("vector", "s_eB", em.val("s_qf"))
                        pst, pco = BANK(qb)
                        em.emit("vector", lambda e, pst=pst, pco=pco: e.tensor_scalar(
                            rotB[0:32, :], pst[32:64, pco:pco + 512], -1.0, None, OP.mult))
                        em.emit("vector", lambda e, pst=pst, pco=pco: e.tensor_copy(
                            rotB[32:64, :], pst[0:32, pco:pco + 512]))
                        em.emit("vector", lambda e, pst=pst, pco=pco: e.tensor_scalar(
                            rotB[64:96, :], pst[96:128, pco:pco + 512], -1.0, None, OP.mult))
                        em.emit("vector", lambda e, pst=pst, pco=pco: e.tensor_copy(
                            rotB[96:128, :], pst[64:96, pco:pco + 512]))
                        em.emit("vector", lambda e, pst=pst, pco=pco, tt=tt: e.tensor_tensor(
                            t1B[:, :], pst[:, pco:pco + 512], cfp[:, tt * 512:(tt + 1) * 512], OP.mult))
                        em.emit("vector", lambda e, tt=tt: e.tensor_tensor(
                            t2B[:, :], rotB[:, :], sfp[:, tt * 512:(tt + 1) * 512], OP.mult))
                        em.emit("vector", lambda e: e.tensor_tensor(t1B[:, :], t1B[:, :], t2B[:, :], OP.add))
                        em.emit("vector", lambda e: e.tensor_scalar(
                            rBs[:, :], t1B[:, :], 0.125, 0.0, OP.mult, OP.max), inc="s_krB")
                        em.wait("scalar", "s_krB", em.val("s_krB"))
                        if em.val("s_eB") >= 1:
                            em.wait("scalar", "s_qf", em.val("s_eB"))
                        em.emit("scalar", lambda e: e.activation(eBs[:, :], t1B[:, :], ACTF.Exp, scale=0.125),
                                inc="s_eB")
                        em.wait("vector", "s_eB")
                        em.emit("vector", lambda e, c=c: e.scalar_tensor_tensor(
                            qf_bf[:, c * 512:(c + 1) * 512], eBs[:, :], 1.0, rBs[:, :], OP.min, OP.add),
                            inc="s_qf")
                    em.wait("tensor", "s_qf")
                    if p + tt > 0:
                        em.wait("tensor", "s_rd", em.val("s_rd"))
                        em.wait("tensor", "s_gsb", em.val("s_gsb"))
                        em.wait("tensor", "s_comb", em.val("s_comb"))
                    for pr in range(8):
                        em.emit("tensor", lambda e, pr=pr: e.matmul(
                            P1[0:16, 0:512], ksbd[:, pr * 16:(pr + 1) * 16],
                            qf_bf[:, pr * 512:(pr + 1) * 512],
                            start=(pr == 0), stop=(pr == 7)),
                            inc="s_den" if pr == 7 else None)
                    em.wait("vector", "s_den")
                    em.emit("vector", lambda e: e.tensor_scalar(rd_f[:, :], P1[0:16, 0:512], 1e-6, None, OP.max))
                    em.emit("vector", lambda e: e.reciprocal(rd_f[:, :], rd_f[:, :]))
                    em.emit("vector", lambda e: e.tensor_copy(rd_bf[:, :], rd_f[:, :]), inc="s_rd")
                    em.wait("tensor", "s_smx", em.val("s_smx"))
                    em.wait("tensor", "s_comb", em.val("s_comb"))
                    for c in range(NC):
                        em.wait("tensor", "s_rd", em.val("s_rd"))
                        if em.val("s_bc") >= 1:
                            em.wait("tensor", "s_bcev", em.val("s_bc"))
                        em.emit("tensor", lambda e, c=c: e.matmul(
                            P1[:, 512:1024], sel16_sb[:, c * 128:(c + 1) * 128], rd_bf[:, :],
                            start=True, stop=True), inc="s_bc")
                        em.wait("scalar", "s_bc", em.val("s_bc"))
                        if em.val("s_bcev") >= 1:
                            em.wait("scalar", "s_qs", em.val("s_bcev"))
                        em.emit("scalar", lambda e: e.copy(bcq_bf[:, :], P1[:, 512:1024]), inc="s_bcev")
                        em.wait("vector", "s_bcev", em.val("s_bcev"))
                        em.emit("vector", lambda e, c=c: e.tensor_tensor(
                            qf_bf[:, c * 512:(c + 1) * 512], qf_bf[:, c * 512:(c + 1) * 512],
                            bcq_bf[:, :], OP.mult), inc="s_qs")
                    em.wait("tensor", "s_qs")
                    em.wait("tensor", "s_kvfin", 1)
                    for pr in range(8):
                        ab = 4 + pr % 2
                        if em.val("s_att") >= 2:
                            em.wait("tensor", "s_attev", em.val("s_att") - 1)
                        pst, pco = BANK(ab)
                        em.emit("tensor", lambda e, pr=pr, pst=pst, pco=pco: e.matmul(
                            pst[:, pco:pco + 512], kvbd[:, pr * 128:(pr + 1) * 128],
                            qf_bf[:, pr * 512:(pr + 1) * 512], start=True, stop=True), inc="s_att")
                        em.wait("scalar", "s_att", em.val("s_att"))
                        em.emit("scalar", lambda e, pr=pr, pst=pst, pco=pco: e.copy(
                            y_sb[:, pr * 512:(pr + 1) * 512], pst[:, pco:pco + 512]), inc="s_attev")
                    em.wait("tensor", "s_attev")
                    em.wait("tensor", "s_wo")
                    for m in range(NC):
                        wb = 6 + m % 2
                        if em.val("s_wt") >= 2:
                            em.wait("tensor", "s_acA", em.val("s_wt") - 1)
                        pst, pco = BANK(wb)
                        for k in range(NC):
                            em.emit("tensor", lambda e, m=m, k=k, pst=pst, pco=pco: e.matmul(
                                pst[:, pco:pco + 512],
                                wq_sb[:, WO_OFF + k * 1024 + m * 128: WO_OFF + k * 1024 + (m + 1) * 128],
                                y_sb[:, k * 512:(k + 1) * 512],
                                start=(k == 0), stop=(k == NC - 1)),
                                inc="s_wt" if k == NC - 1 else None)
                        em.wait("vector", "s_wt", em.val("s_wt"))
                        em.wait("vector", "s_accx")
                        em.emit("vector", lambda e, m=m, tt=tt, pst=pst, pco=pco: e.tensor_tensor(
                            accb[:, m * TP + tt * 512: m * TP + (tt + 1) * 512],
                            accb[:, m * TP + tt * 512: m * TP + (tt + 1) * 512],
                            pst[:, pco:pco + 512], OP.add), inc="s_acA")
                    em.wait("tensor", "s_wr")
                    em.wait("tensor", "s_rd", em.val("s_rd"))
                    if em.val("s_comb") >= 1:
                        em.wait("tensor", "s_comb", em.val("s_comb"))
                    for k in range(NC):
                        em.emit("tensor", lambda e, k=k, tt=tt: e.matmul(
                            P1[0:8, 0:512], wr_sb[:, k * 8:(k + 1) * 8],
                            xnp[:, k * TP + tt * 512: k * TP + (tt + 1) * 512],
                            start=(k == 0), stop=(k == NC - 1)),
                            inc="s_rt" if k == NC - 1 else None)
                    em.wait("vector", "s_rt")
                    em.emit("vector", lambda e: e.tensor_copy(gsb[:, :], P1[0:8, 0:512]), inc="s_gsb")
                    em.wait("tensor", "s_gsb")
                    em.wait("tensor", "s_bcev", em.val("s_bcev"))
                    for j in range(4):
                        em.emit("tensor", lambda e, j=j: e.transpose(
                            P1[:, 512 + j * 8: 512 + (j + 1) * 8],
                            gsb[:, j * 128:(j + 1) * 128], idf_sb[0:8, 0:8]),
                            inc="s_trf" if j == 3 else None)
                    em.wait("vector", "s_trf")
                    for j in range(4):
                        gj = (P1, 512 + j * 8)
                        em.emit("vector", lambda e, j=j, gj=gj: e.tensor_reduce(
                            m1s[:, 2 * j:2 * j + 1], gj[0][:, gj[1]:gj[1] + 8], AX, OP.max))
                        em.emit("vector", lambda e, j=j, gj=gj: e.tensor_reduce(
                            m1s[:, 2 * j + 1:2 * j + 2], gj[0][:, gj[1]:gj[1] + 8], AX, OP.max,
                            negate=True), inc="s_smx" if j == 3 else None)
                    em.wait("scalar", "s_smx", em.val("s_smx"))
                    for j in range(4):
                        gj = (P1, 512 + j * 8)
                        em.emit("scalar", lambda e, j=j, gj=gj: e.activation(
                            es_all[:, j * 8:(j + 1) * 8], gj[0][:, gj[1]:gj[1] + 8], ACTF.Exp,
                            bias=m1s[:, 2 * j + 1:2 * j + 2]), inc="s_smx" if j == 3 else None)
                    em.wait("vector", "s_smx", em.val("s_smx"))
                    for j in range(4):
                        em.emit("vector", lambda e, j=j: e.tensor_reduce(
                            sums8[:, j:j + 1], es_all[:, j * 8:(j + 1) * 8], AX, OP.add))
                    for j in range(4):
                        gj = (P1, 512 + j * 8)
                        em.emit("vector", lambda e, j=j, gj=gj: e.tensor_scalar(
                            pen_all[:, j * 8:(j + 1) * 8], gj[0][:, gj[1]:gj[1] + 8],
                            m1s[:, 2 * j:2 * j + 1], None, OP.is_ge))
                    for j in range(4):
                        em.emit("vector", lambda e, j=j: e.tensor_tensor(
                            pba[:, j * 8:(j + 1) * 8], pen_all[:, j * 8:(j + 1) * 8],
                            bigneg[:, :], OP.mult))
                    for j in range(4):
                        gj = (P1, 512 + j * 8)
                        em.emit("vector", lambda e, j=j, gj=gj: e.tensor_tensor(
                            msk_all[:, j * 8:(j + 1) * 8], pba[:, j * 8:(j + 1) * 8],
                            gj[0][:, gj[1]:gj[1] + 8], OP.add))
                    for j in range(4):
                        em.emit("vector", lambda e, j=j: e.tensor_reduce(
                            m2s8[:, j:j + 1], msk_all[:, j * 8:(j + 1) * 8], AX, OP.max))
                    for j in range(4):
                        gj = (P1, 512 + j * 8)
                        em.emit("vector", lambda e, j=j, gj=gj: e.tensor_tensor(
                            es2_all[:, j * 8:(j + 1) * 8], gj[0][:, gj[1]:gj[1] + 8],
                            ap3(m2s8, j, [[4, 128], [0, 8]]), OP.is_ge))
                    for j in range(4):
                        em.emit("vector", lambda e, j=j: e.tensor_tensor(
                            combtok[:, j * 8:(j + 1) * 8], es_all[:, j * 8:(j + 1) * 8],
                            es2_all[:, j * 8:(j + 1) * 8], OP.mult),
                            inc="s_smx" if j == 3 else None)
                    em.wait("tensor", "s_smx", em.val("s_smx"))
                    for j in range(4):
                        em.emit("tensor", lambda e, j=j: e.transpose(
                            P1[0:8, j * 128:(j + 1) * 128], combtok[:, j * 8:(j + 1) * 8],
                            idf_sb[:, :]))
                    for j in range(4):
                        em.emit("tensor", lambda e, j=j: e.transpose(
                            P1[0:1, 512 + j * 128: 512 + (j + 1) * 128], sums8[:, j:j + 1],
                            idf_sb[:, :]), inc="s_trb" if j == 3 else None)
                    em.wait("vector", "s_trb")
                    em.emit("vector", lambda e: e.tensor_copy(cu_sb[:, :], P1[0:8, 0:512]))
                    em.emit("vector", lambda e: e.tensor_copy(srow_sb[:, :], P1[0:1, 512:1024]),
                            inc="s_rsv")
                    em.wait("scalar", "s_rsv", em.val("s_rsv"))
                    em.emit("scalar", lambda e: e.activation(rs_sb[0:1, :], srow_sb[0:1, :], ACTF.Ln))
                    em.emit("scalar", lambda e: e.activation(
                        rs_sb[0:1, :], rs_sb[0:1, :], ACTF.Exp, scale=-1.0), inc="s_rsv")
                    em.wait("tensor", "s_rsv", em.val("s_rsv"))
                    em.emit("tensor", lambda e: e.matmul(
                        P1[0:8, 512:1024], ones1_sb[0:1, 0:8], rs_sb[0:1, :],
                        start=True, stop=True), inc="s_trb")
                    em.wait("vector", "s_trb", em.val("s_trb"))
                    em.emit("vector", lambda e, tt=tt: e.tensor_tensor(
                        comb_bf[:, tt * 512:(tt + 1) * 512], cu_sb[:, :], P1[0:8, 512:1024],
                        OP.mult), inc="s_comb")
                    if _DBG and p == 0 and tt == 0:
                        em.dma("sync", "s_dbg", lambda e: e.dma_start(DBG_G.ap(), gsb[:, :]))
                        em.dma("sync", "s_dbg", lambda e: e.dma_start(DBG_CT.ap(), combtok[:, :]))
                        em.wait("sync", "s_comb", em.val("s_comb"))
                        em.wait("sync", "s_gsb", em.val("s_gsb"))
                    if _DBG and p == 0 and tt == NTT - 1:
                        em.wait("sync", "s_comb", em.val("s_comb"))
                        em.wait("gpsimd", "s_comb", em.val("s_comb"))
                        em.dma("gpsimd", "s_dbg", lambda e: e.dma_start(
                            ap3(DBG_CB, 0, [[NTT * 512, 8], [1, NTT * 512]]),
                            ap3(comb_bf, 0, [[NTT * 512, 8], [1, NTT * 512]])))

                em.wait("tensor", "s_acA", em.val("s_acA"))
                for e_i in range(8):
                    em.wait("tensor", "s_comb", em.val("s_comb"))
                    for tt in range(NTT):
                        if em.val("s_bcePE") >= 1:
                            em.wait("tensor", "s_bceS", em.val("s_bcePE"))
                        em.emit("tensor", lambda e, e_i=e_i, tt=tt: e.matmul(
                            P2[:, 0:512], sel8_sb[:, e_i * 128:(e_i + 1) * 128],
                            comb_bf[:, tt * 512:(tt + 1) * 512], start=True, stop=True), inc="s_bcePE")
                        em.wait("scalar", "s_bcePE", em.val("s_bcePE"))
                        if em.val("s_bceS") >= 2:
                            em.wait("scalar", "s_y", hmm_cnt)
                        em.emit("scalar", lambda e, tt=tt: e.copy(
                            bce_sb[:, tt * 512:(tt + 1) * 512], P2[:, 0:512]), inc="s_bceS")
                    for half in range(2):
                        gh = e_i * 2 + half
                        buf = gh % 2
                        if w1_last_use[buf] > 0:
                            em.wait("gpsimd", "s_gmm", w1_last_use[buf])
                        if gh == 1:
                            em.wait("gpsimd", "s_wt", em.val("s_wt"))
                        em.dma("gpsimd", f"s_w1g{buf}", lambda e, e_i=e_i, half=half, buf=buf: e.dma_start(
                            ap3(w1h[buf], 0, [[16384, 128], [2048, NC], [1, 1024]]),
                            ap3(W1T, e_i * D * 4 * D + half * 1024,
                                [[4 * D, 128], [128 * 4 * D, NC], [1, 1024]])))
                        em.dma("gpsimd", f"s_w1u{buf}", lambda e, e_i=e_i, half=half, buf=buf: e.dma_start(
                            ap3(w1h[buf], 1024, [[16384, 128], [2048, NC], [1, 1024]]),
                            ap3(W1T, e_i * D * 4 * D + 2048 + half * 1024,
                                [[4 * D, 128], [128 * 4 * D, NC], [1, 1024]])))
                        if gh >= 1 or p > 0:
                            em.wait("gpsimd", "s_w2mm", w2mm_cnt)
                        em.dma("gpsimd", "s_w2", lambda e, e_i=e_i, half=half: e.dma_start(
                            ap3(w2h, 0, [[8 * 1024, 128], [1024, NC], [1, 1024]]),
                            ap3(W2T, e_i * 2 * D * D + half * 1024 * D,
                                [[D, 128], [128 * D, NC], [1, 1024]])))

                        em.wait("tensor", f"s_w1g{buf}")
                        em.wait("tensor", f"s_w1u{buf}")
                        em.wait("tensor", "s_bceS", em.val("s_bceS"))
                        for pr in range(8):
                            for tt in range(NTT):
                                pi = hmm_cnt % 2
                                gb, ub = (0, 1) if pi == 0 else (2, 3)
                                if hmm_cnt >= 2:
                                    em.wait("tensor", "s_y", hmm_cnt - 1)
                                gst, gco = BANK(gb)
                                ust, uco = BANK(ub)
                                for k in range(NC):
                                    em.emit("tensor", lambda e, pr=pr, tt=tt, k=k, gst=gst, gco=gco, buf=buf: e.matmul(
                                        gst[:, gco:gco + 512],
                                        w1h[buf][:, (k * 2 + 0) * 1024 + pr * 128: (k * 2 + 0) * 1024 + (pr + 1) * 128],
                                        xnp[:, k * TP + tt * 512: k * TP + (tt + 1) * 512],
                                        start=(k == 0), stop=(k == NC - 1)))
                                for k in range(NC):
                                    em.emit("tensor", lambda e, pr=pr, tt=tt, k=k, ust=ust, uco=uco, buf=buf: e.matmul(
                                        ust[:, uco:uco + 512],
                                        w1h[buf][:, (k * 2 + 1) * 1024 + pr * 128: (k * 2 + 1) * 1024 + (pr + 1) * 128],
                                        xnp[:, k * TP + tt * 512: k * TP + (tt + 1) * 512],
                                        start=(k == 0), stop=(k == NC - 1)),
                                        inc="s_gmm" if k == NC - 1 else None)
                                hmm_cnt += 1
                                em.wait("scalar", "s_gmm", hmm_cnt)
                                em.wait("scalar", "s_y", em.val("s_y"))
                                em.emit("scalar", lambda e, gst=gst, gco=gco: e.activation(
                                    silu_sb[:, :], gst[:, gco:gco + 512], ACTF.Silu), inc="s_silu")
                                em.wait("vector", "s_silu", em.val("s_silu"))
                                em.emit("vector", lambda e, ust=ust, uco=uco: e.tensor_tensor(
                                    tmf[:, :], silu_sb[:, :], ust[:, uco:uco + 512], OP.mult))
                                em.emit("vector", lambda e, pr=pr, tt=tt: e.tensor_tensor(
                                    y_sb[:, (pr * NTT + tt) * 512: (pr * NTT + tt + 1) * 512],
                                    tmf[:, :], bce_sb[:, tt * 512:(tt + 1) * 512], OP.mult), inc="s_y")
                        w1_last_use[buf] = em.val("s_gmm")
                        em.wait("tensor", "s_y", em.val("s_y"))
                        em.wait("tensor", "s_w2")
                        for tt in range(NTT):
                            for m in range(NC):
                                wb = 6 + w2mm_cnt % 2
                                if w2mm_cnt >= 2:
                                    em.wait("tensor", "s_acF", w2mm_cnt - 1)
                                pst, pco = BANK(wb)
                                for k in range(8):
                                    em.emit("tensor", lambda e, tt=tt, m=m, k=k, pst=pst, pco=pco: e.matmul(
                                        pst[:, pco:pco + 512],
                                        w2h[:, k * 1024 + m * 128: k * 1024 + (m + 1) * 128],
                                        y_sb[:, (k * NTT + tt) * 512: (k * NTT + tt + 1) * 512],
                                        start=(k == 0), stop=(k == 7)),
                                        inc="s_w2mm" if k == 7 else None)
                                w2mm_cnt += 1
                                em.wait("vector", "s_w2mm", w2mm_cnt)
                                em.emit("vector", lambda e, tt=tt, m=m, pst=pst, pco=pco: e.tensor_tensor(
                                    accb[:, m * TP + tt * 512: m * TP + (tt + 1) * 512],
                                    accb[:, m * TP + tt * 512: m * TP + (tt + 1) * 512],
                                    pst[:, pco:pco + 512], OP.add), inc="s_acF")

                em.wait("sync", "s_acF", em.val("s_acF"))
                em.dma("sync", "s_out", lambda e, p=p: e.dma_start(
                    ap3(OUT, p * TP, [[TOWN, 128], [128 * TOWN, NC], [1, TP]]),
                    ap3(accb, 0, [[NC * TP, 128], [TP, NC], [1, TP]])))
                if p < NPASS - 1:
                    em.wait("gpsimd", "s_gmm", em.val("s_gmm"))
                    load_wqwo()
                em.nop_inc("tensor", "sPB")
                em.nop_inc("vector", "sVB")

            if _DBG:
                em.wait("vector", "s_smx", 3)
                em.dma("sync", "s_dbg", lambda e: e.dma_start(DBG_ES.ap(), dbges[:, :]))
                em.dma("sync", "s_dbg", lambda e: e.dma_start(DBG_SUM.ap(), dbgsum[:, :]))
            em.wait("sync", "s_out", 16 * NPASS)
            if _DBG:
                em.wait("sync", "s_dbg", em.val("s_dbg"))

            with nc.Block() as blk:
                em.replay(blk)

    return nc



def _tables():
    invf = (1.0 / (500000.0 ** (np.arange(0, HD, 2, dtype=np.float64) / HD))).astype(np.float64)
    t = np.arange(T, dtype=np.float64)
    ang = t[:, None] * invf[None, :]
    emb = np.concatenate([ang, ang], axis=1)
    return np.cos(emb).astype(np.float32), np.sin(emb).astype(np.float32)


def _consts():
    sel16 = np.zeros((16, NC * 128), dtype=ml_dtypes.bfloat16)
    for c in range(NC):
        for m in range(128):
            sel16[2 * c + m // 64, c * 128 + m] = 1.0
    sel8 = np.zeros((8, 8 * 128), dtype=ml_dtypes.bfloat16)
    for e in range(8):
        sel8[e, e * 128:(e + 1) * 128] = 1.0
    ones1 = np.ones((1, 128), dtype=np.float32)
    onesb = np.ones((128, 1), dtype=ml_dtypes.bfloat16)
    idf = np.eye(128, dtype=np.float32)
    epsv = np.full((128, 1), 1e-5, dtype=np.float32)
    return sel16, sel8, ones1, onesb, idf, epsv


def kernel(L0_attn_x, L0_attn_W_qkv, L0_attn_W_out, L0_ffn_w1, L0_ffn_w2, L0_ffn_router_w):
    x = np.asarray(L0_attn_x, dtype=np.float32)
    wqkv = np.ascontiguousarray(np.asarray(L0_attn_W_qkv, dtype=np.float32))
    wout = np.ascontiguousarray(np.asarray(L0_attn_W_out, dtype=np.float32))
    w1 = np.asarray(L0_ffn_w1, dtype=np.float32)
    w2 = np.asarray(L0_ffn_w2, dtype=np.float32)
    wr = np.asarray(L0_ffn_router_w, dtype=np.float32)

    w1t = np.ascontiguousarray(w1.transpose(0, 2, 1))
    w2t = np.ascontiguousarray(w2.transpose(0, 2, 1))
    wrt = np.ascontiguousarray(wr.T)

    cos64, sin64 = _tables()
    sel16, sel8, ones1, onesb, idf, epsv = _consts()

    in_maps = []
    for c in range(8):
        b, h = c // 2, c % 2
        xT = np.ascontiguousarray(x[b].T)
        own = xT[:, h * TOWN:(h + 1) * TOWN]
        oth = xT[:, (1 - h) * TOWN:(2 - h) * TOWN]
        x_core = np.ascontiguousarray(np.concatenate([own, oth], axis=1))
        order = np.concatenate([np.arange(h * TOWN, (h + 1) * TOWN),
                                np.arange((1 - h) * TOWN, (2 - h) * TOWN)])
        cost = np.ascontiguousarray(cos64[order])
        sint = np.ascontiguousarray(sin64[order])
        cf = cos64[h * TOWN:(h + 1) * TOWN, :].T
        sf = sin64[h * TOWN:(h + 1) * TOWN, :].T
        cosf = np.ascontiguousarray(np.concatenate([cf, cf], axis=0))
        sinf = np.ascontiguousarray(np.concatenate([sf, sf], axis=0))
        in_maps.append(dict(
            x=x_core, wqkv=wqkv, wout=wout, w1t=w1t, w2t=w2t, wrt=wrt,
            cost=cost, sint=sint, cosf=cosf, sinf=sinf,
            sel16=sel16, sel8=sel8, ones1=ones1, onesb=onesb, idf=idf, epsv=epsv))

    nc = build_program()
    res = run_bass_kernel_spmd(nc, in_maps, core_ids=list(range(8)))

    out = np.empty((4, T, D), dtype=np.float32)
    for c in range(8):
        b, h = c // 2, c % 2
        out[b, h * TOWN:(h + 1) * TOWN, :] = res.results[c]["out"].T
    return out
